# version 06
# baseline (speedup 1.0000x reference)
import contextlib
import math

import numpy as np
import ml_dtypes

import concourse.bass as bass
import concourse.mybir as mybir
from concourse.bass_utils import run_bass_kernel_spmd

F32 = mybir.dt.float32
BF16 = mybir.dt.bfloat16
ALU = mybir.AluOpType
ACT = mybir.ActivationFunctionType
AX = mybir.AxisListType

D = 2048
H = 16
KVH = 4
HD = 128
RD = 64
CF = 512
FH = 5632
LN_EPS = 1e-5
P1H = 99
P1QC = 99
P1CUT = 99
P2BCUT = 99
EXP_SCALE = float(np.float32(1.0) / np.sqrt(np.float32(HD + RD)))


class KB:
    ENG = ("t", "v", "s", "g", "y")

    def __init__(self, nc, st):
        self.nc = nc
        self.st = st
        self.eng = {"t": nc.tensor, "v": nc.vector, "s": nc.scalar,
                    "g": nc.gpsimd, "y": nc.sync}
        self.sems = {}
        self.cnt = {}
        self.waited = {e: {} for e in self.ENG}
        self.state = {}
        self.pending_reads = []
        self.dma_slots = {}
        self.phase = 0
        self.barrier_snapshot = None
        self.barrier_done = set()

    def _sem(self, key):
        if key not in self.sems:
            self.sems[key] = self.st.enter_context(
                self.nc.semaphore(f"sm{key[0]}{key[1]}{key[2]}"))
            self.cnt[key] = 0
        return self.sems[key]

    def new_phase(self):
        self.barrier_snapshot = {k: v for k, v in self.cnt.items() if v > 0}
        self.phase += 1
        self.barrier_done = set()
        self.state = {}

    def _apply_barrier(self, e):
        if self.barrier_snapshot is None or e in self.barrier_done:
            return
        self.barrier_done.add(e)
        for kk, v in self.barrier_snapshot.items():
            self._wait(e, kk, v)

    def _wait(self, e, semkey, val):
        if val <= 0:
            return
        w = self.waited[e]
        if w.get(semkey, 0) >= val:
            return
        w[semkey] = val
        self.eng[e].wait_ge(self._sem(semkey), val)

    def _dep_waits(self, e, reads, writes):
        self._apply_barrier(e)
        need = {}
        for b in reads:
            s = self.state.get(b)
            if s and s["w"]:
                kk, v = s["w"]
                need[kk] = max(need.get(kk, 0), v)
        for b in writes:
            s = self.state.get(b)
            if s:
                if s["w"]:
                    kk, v = s["w"]
                    need[kk] = max(need.get(kk, 0), v)
                for kk, v in s["r"].items():
                    need[kk] = max(need.get(kk, 0), v)
        for kk, v in need.items():
            self._wait(e, kk, v)

    def _note(self, semkey, val, reads, writes):
        for b in writes:
            self.state[b] = {"w": (semkey, val), "r": {}}
        for b in reads:
            s = self.state.setdefault(b, {"w": None, "r": {}})
            s["r"][semkey] = max(s["r"].get(semkey, 0), val)

    def op(self, e, fn, reads=(), writes=()):
        reads, writes = list(reads), list(writes)
        self._dep_waits(e, reads, writes)
        instr = fn(self.eng[e])
        kk = ("e", e, self.phase)
        self._sem(kk)
        self.cnt[kk] += 1
        instr.then_inc(self.sems[kk], 1)
        self._note(kk, self.cnt[kk], reads, writes)
        return instr

    def mm(self, psum_ap, lhsT_ap, rhs_ap, reads, psum_buf, start, stop,
           inc=False):
        reads = list(reads)
        self._dep_waits("t", reads, [psum_buf] if start else [])
        instr = self.eng["t"].matmul(psum_ap, lhsT_ap, rhs_ap,
                                     start=start, stop=stop)
        kk = ("e", "t", self.phase)
        self._sem(kk)
        if stop or inc:
            self.cnt[kk] += 1
            instr.then_inc(self.sems[kk], 1)
            self._note(kk, self.cnt[kk], reads,
                       [psum_buf] if stop else [])
        else:
            self._note(kk, self.cnt[kk] + 1, reads, [])
        return instr

    def transpose(self, psum_ap, in_ap, ident_ap, reads, psum_buf):
        reads = list(reads)
        self._dep_waits("t", reads, [psum_buf])
        instr = self.eng["t"].transpose(psum_ap, in_ap, ident_ap)
        kk = ("e", "t", self.phase)
        self._sem(kk)
        self.cnt[kk] += 1
        instr.then_inc(self.sems[kk], 1)
        self._note(kk, self.cnt[kk], reads, [psum_buf])
        return instr

    def dma(self, q, out_ap, in_ap, reads=(), writes=()):
        reads, writes = list(reads), list(writes)
        assert writes, "every DMA must declare a destination buffer key"
        self._dep_waits(q, reads, writes)
        instr = self.eng[q].dma_start(out=out_ap, in_=in_ap)
        slot = self.dma_slots.setdefault((self.phase, writes[0]),
                                         len(self.dma_slots))
        kk = ("d", slot, self.phase)
        self._sem(kk)
        self.cnt[kk] += 16
        instr.then_inc(self.sems[kk], 16)
        self._note(kk, self.cnt[kk], reads, writes)
        return instr

    def wait_all_on(self, e):
        for kk, v in list(self.cnt.items()):
            if v > 0:
                self._wait(e, kk, v)


def build_core_program(S, upto=99):
    TC = 256
    NTC = S // TC
    NQC = S // 128
    NSUB = S // 128
    NFC = 2 * FH // 512
    NFH = FH // 128
    nc = bass.Bass("TRN2", target_bir_lowering=False, debug=False)

    def din(name, shape, dt=F32):
        return nc.dram_tensor(name, list(shape), dt, kind="ExternalInput").ap()

    x_in = din("x", (S, D))
    wq_in = din("wq", (D, H * HD))
    wqr_in = din("wqr", (D, H * RD))
    wkvc_in = din("wkvc", (D, CF))
    wkdec_in = din("wkdec", (CF, KVH * HD))
    wvdec_in = din("wvdec", (CF, KVH * HD))
    wkr_in = din("wkr", (CF, KVH * RD))
    wo_in = din("wo", (H * HD, D))
    w13t_in = din("w13t", (D, 2 * FH))
    w2t_in = din("w2t", (FH, D))
    nwb_in = din("nwb", (128, FH))
    ident_in = din("ident", (128, 128))
    identb_in = din("identb", (128, 128), BF16)
    dmask_in = din("dmask", (512, 512))
    ones_in = din("ones", (128, 128))
    m1q_in = din("m1q", (128, H * RD))
    m2q_in = din("m2q", (128, H * RD))
    m1k_in = din("m1k", (128, KVH * RD))
    m2k_in = din("m2k", (128, KVH * RD))
    out_t = nc.dram_tensor("out", [S, D], F32, kind="ExternalOutput").ap()

    def scr(name, shape, dt=BF16):
        return nc.dram_tensor(name, list(shape), dt, kind="Internal").ap()

    s_wq = [scr("s_wq_h", (D, H * HD)), scr("s_wq_l", (D, H * HD))]
    s_wqr = [scr("s_wqr_h", (D, H * RD)), scr("s_wqr_l", (D, H * RD))]
    s_wkvc = [scr("s_wkvc_h", (D, CF)), scr("s_wkvc_l", (D, CF))]
    s_wkdec = [scr("s_wkdec_h", (CF, KVH * HD)), scr("s_wkdec_l", (CF, KVH * HD))]
    s_wvdec = [scr("s_wvdec_h", (CF, KVH * HD)), scr("s_wvdec_l", (CF, KVH * HD))]
    s_wkr = [scr("s_wkr_h", (CF, KVH * RD)), scr("s_wkr_l", (CF, KVH * RD))]
    s_wo = [scr("s_wo_h", (H * HD, D)), scr("s_wo_l", (H * HD, D))]
    s_qn = [scr("s_qn_h", (H * HD, S)), scr("s_qn_l", (H * HD, S))]
    s_qr = [scr("s_qr_h", (H * RD, S)), scr("s_qr_l", (H * RD, S))]
    s_ot = [scr("s_ot_h", (H * HD, S)), scr("s_ot_l", (H * HD, S))]
    s_x2 = scr("s_x2", (S, D), F32)
    s_gnw = scr("s_gnw", (S, FH), F32)

    st = contextlib.ExitStack()
    with st:
        E = st.enter_context
        k = KB(nc, st)

        ident = E(nc.sbuf_tensor("sb_ident", [128, 128], F32))
        identb = E(nc.sbuf_tensor("sb_identb", [128, 128], BF16))
        dmask = E(nc.sbuf_tensor("sb_dmask", [128, 4, 512], F32))
        ones = E(nc.sbuf_tensor("sb_ones", [128, 128], F32))
        c127 = E(nc.sbuf_tensor("c127", [128, 1], F32))
        ceps = E(nc.sbuf_tensor("ceps", [128, 1], F32))
        ctx01 = contextlib.ExitStack()
        kn_sb = [ctx01.enter_context(nc.sbuf_tensor(f"kn_{p}", [128, KVH, S], BF16)) for p in "hl"]
        kr_sb = [ctx01.enter_context(nc.sbuf_tensor(f"kr_{p}", [128, KVH // 2, S], BF16)) for p in "hl"]
        v_sb = [ctx01.enter_context(nc.sbuf_tensor(f"v_{p}", [128, S // 128, KVH * HD], BF16)) for p in "hl"]

        k.dma("y", ident[:], ident_in, writes=["ident"])
        k.dma("y", identb[:], identb_in, writes=["identb"])
        k.dma("y", dmask[:], dmask_in.rearrange("(a p) c -> p a c", p=128), writes=["dmask"])
        k.dma("y", ones[:], ones_in, writes=["ones"])
        k.op("v", lambda e: e.memset(c127[:, 0:1], 127.0), writes=["c127"])
        k.op("v", lambda e: e.memset(ceps[:, 0:1], LN_EPS), writes=["ceps"])

        def split_evac(psum_ap, psum_buf, hi_ap, lo_ap, hi_buf, lo_buf):
            k.op("v", lambda e: e.tensor_copy(out=hi_ap, in_=psum_ap),
                 reads=[psum_buf], writes=[hi_buf])
            k.op("v", lambda e: e.tensor_tensor(
                out=lo_ap, in0=psum_ap, in1=hi_ap, op=ALU.subtract),
                reads=[psum_buf, hi_buf], writes=[lo_buf])

        def phase_pre():
            ctx = contextlib.ExitStack()
            C = ctx.enter_context
            spl_f = C(nc.sbuf_tensor("spl_f", [128, 4096], F32))
            spl_h = C(nc.sbuf_tensor("spl_h", [128, 4096], BF16))
            spl_l = C(nc.sbuf_tensor("spl_l", [128, 4096], BF16))
            spl_t = C(nc.sbuf_tensor("spl_t", [128, 1024], F32))
            spl_u = C(nc.sbuf_tensor("spl_u", [128, 1024], F32))
            m1q_sb = C(nc.sbuf_tensor("m1q_sb", [128, H * RD], F32))
            m2q_sb = C(nc.sbuf_tensor("m2q_sb", [128, H * RD], F32))
            m1k_sb = C(nc.sbuf_tensor("m1k_sb", [128, KVH * RD], F32))
            m2k_sb = C(nc.sbuf_tensor("m2k_sb", [128, KVH * RD], F32))
            k.dma("y", m1q_sb[:], m1q_in, writes=["m1q"])
            k.dma("y", m2q_sb[:], m2q_in, writes=["m2q"])
            k.dma("y", m1k_sb[:], m1k_in, writes=["m1k"])
            k.dma("y", m2k_sb[:], m2k_in, writes=["m2k"])
            specs = [(wq_in, s_wq, D, H * HD, None),
                     (wqr_in, s_wqr, D, H * RD, (m1q_sb, m2q_sb, "m1q", "m2q")),
                     (wkvc_in, s_wkvc, D, CF, None),
                     (wkdec_in, s_wkdec, CF, KVH * HD, None),
                     (wvdec_in, s_wvdec, CF, KVH * HD, None),
                     (wkr_in, s_wkr, CF, KVH * RD, (m1k_sb, m2k_sb, "m1k", "m2k")),
                     (wo_in, s_wo, H * HD, D, None)]
            for w_in, w_scr, R, C, ropem in specs:
                for r0 in range(0, R, 128):
                    for c0 in range(0, C, 4096):
                        cw = min(4096, C - c0)
                        k.dma("y", spl_f[:, 0:cw], w_in[r0:r0 + 128, c0:c0 + cw],
                              writes=["spl_f"])
                        if ropem is not None:
                            m1t, m2t, m1b, m2b = ropem
                            nb = cw // 64
                            w4 = spl_f[:, 0:cw].rearrange(
                                "p (b h f) -> p b h f", h=2, f=32)
                            t4 = spl_u[:, 0:cw].rearrange(
                                "p (b h f) -> p b h f", h=2, f=32)
                            m24 = m2t[:, 0:cw].rearrange(
                                "p (b h f) -> p b h f", h=2, f=32)
                            k.op("v", lambda e, cw=cw, m1t=m1t: e.tensor_tensor(
                                out=spl_t[:, 0:cw], in0=spl_f[:, 0:cw],
                                in1=m1t[:, 0:cw], op=ALU.mult),
                                reads=["spl_f", m1b], writes=["spl_t"])
                            k.op("v", lambda e, w4=w4, t4=t4, m24=m24:
                                 e.tensor_tensor(
                                     out=t4[:, :, 0, :], in0=w4[:, :, 1, :],
                                     in1=m24[:, :, 0, :], op=ALU.mult),
                                 reads=["spl_f", m2b], writes=["spl_u"])
                            k.op("v", lambda e, w4=w4, t4=t4, m24=m24:
                                 e.tensor_tensor(
                                     out=t4[:, :, 1, :], in0=w4[:, :, 0, :],
                                     in1=m24[:, :, 1, :], op=ALU.mult),
                                 reads=["spl_f", m2b], writes=["spl_u"])
                            k.op("v", lambda e, cw=cw: e.tensor_tensor(
                                out=spl_f[:, 0:cw], in0=spl_t[:, 0:cw],
                                in1=spl_u[:, 0:cw], op=ALU.add),
                                reads=["spl_t", "spl_u"], writes=["spl_f"])
                        k.op("v", lambda e, cw=cw: e.tensor_copy(
                            out=spl_h[:, 0:cw], in_=spl_f[:, 0:cw]),
                            reads=["spl_f"], writes=["spl_h"])
                        k.op("v", lambda e, cw=cw: e.tensor_sub(
                            spl_l[:, 0:cw], spl_f[:, 0:cw], spl_h[:, 0:cw]),
                            reads=["spl_f", "spl_h"], writes=["spl_l"])
                        k.dma("s", w_scr[0][r0:r0 + 128, c0:c0 + cw],
                              spl_h[:, 0:cw], reads=["spl_h"], writes=[("scr", id(w_scr[0]))])
                        k.dma("s", w_scr[1][r0:r0 + 128, c0:c0 + cw],
                              spl_l[:, 0:cw], reads=["spl_l"], writes=[("scr", id(w_scr[1]))])
            ctx.close()

        if upto >= 0:
            phase_pre()
        k.new_phase()

        def phase_p0():
            ctx = contextlib.ExitStack()
            C = ctx.enter_context
            xg = C(nc.sbuf_tensor("xg", [128, 2, D], F32))
            xgh = C(nc.sbuf_tensor("xgh", [128, 2, D], BF16))
            xgl = C(nc.sbuf_tensor("xgl", [128, 2, D], BF16))
            xt_h = C(nc.sbuf_tensor("xt_h", [128, D // 128, TC], BF16))
            xt_l = C(nc.sbuf_tensor("xt_l", [128, D // 128, TC], BF16))
            ct_h = C(nc.sbuf_tensor("ct_h", [128, CF // 128, TC], BF16))
            ct_l = C(nc.sbuf_tensor("ct_l", [128, CF // 128, TC], BF16))
            wa_h = C(nc.sbuf_tensor("wa_h", [128, 16, 512], BF16))
            wa_l = C(nc.sbuf_tensor("wa_l", [128, 16, 512], BF16))
            wb_h = C(nc.sbuf_tensor("wb_h", [128, 4, 512], BF16))
            wb_l = C(nc.sbuf_tensor("wb_l", [128, 4, 512], BF16))
            ev_h = C(nc.sbuf_tensor("ev_h", [128, TC], BF16))
            ev_l = C(nc.sbuf_tensor("ev_l", [128, TC], BF16))
            pT = C(nc.psum_tensor("pT", [128, 512], F32))
            pT2 = C(nc.psum_tensor("pT2", [128, 512], F32))
            pA = C(nc.psum_tensor("pA", [128, 512], F32))
            pB = C(nc.psum_tensor("pB", [128, 512], F32))


            xt_reads = [("xt_h", dc) for dc in range(16)] + \
                       [("xt_l", dc) for dc in range(16)]
            ct_reads = [("ct_h", c) for c in range(4)] + \
                       [("ct_l", c) for c in range(4)]

            for tc in range(NTC):
                t0 = tc * TC
                for sub in range(2):
                    k.dma("y", xg[:, sub, :],
                          x_in[t0 + sub * 128:t0 + (sub + 1) * 128, :],
                          writes=[("xg", sub)])
                for sub in range(2):
                    k.op("v", lambda e, sub=sub: e.tensor_copy(
                        out=xgh[:, sub, :], in_=xg[:, sub, :]),
                        reads=[("xg", sub)], writes=[("xgh", sub)])
                    k.op("v", lambda e, sub=sub: e.tensor_tensor(
                        out=xgl[:, sub, :], in0=xg[:, sub, :],
                        in1=xgh[:, sub, :], op=ALU.subtract),
                        reads=[("xg", sub), ("xgh", sub)], writes=[("xgl", sub)])
                jrot = 0
                for sub in range(2):
                    for dc in range(D // 128):
                        for src_t, srcb, dst_t, dstb in (
                                (xgh, ("xgh", sub), xt_h, ("xt_h", dc)),
                                (xgl, ("xgl", sub), xt_l, ("xt_l", dc))):
                            bank, bkey = ((pT, "pT0") if jrot % 2 == 0
                                          else (pT2, "pT1"))
                            jrot += 1
                            k.mm(bank[:, 0:128],
                                 src_t[:, sub, dc * 128:(dc + 1) * 128],
                                 identb[:], [srcb, "identb"], bkey, True, True)
                            k.op("v", lambda e, bank=bank, dst_t=dst_t, dc=dc,
                                 sub=sub: e.tensor_copy(
                                     out=dst_t[:, dc, sub * 128:(sub + 1) * 128],
                                     in_=bank[:, 0:128]),
                                 reads=[bkey], writes=[dstb])

                k.dma("y", wa_h[:, :, :],
                      s_wkvc[0].rearrange("(a p) c -> p a c", p=128),
                      writes=["wa_h"])
                k.dma("y", wa_l[:, :, :],
                      s_wkvc[1].rearrange("(a p) c -> p a c", p=128),
                      writes=["wa_l"])
                for cf in range(4):
                    for d in range(16):
                        lhh = wa_h[:, d, cf * 128:(cf + 1) * 128]
                        lhl = wa_l[:, d, cf * 128:(cf + 1) * 128]
                        k.mm(pA[:, 0:TC], lhh, xt_h[:, d, :], ["wa_h"] + xt_reads,
                             "pA", d == 0, False)
                        k.mm(pA[:, 0:TC], lhh, xt_l[:, d, :], ["wa_h"] + xt_reads,
                             "pA", False, False)
                        k.mm(pA[:, 0:TC], lhl, xt_h[:, d, :], ["wa_l"] + xt_reads,
                             "pA", False, d == 15)
                    split_evac(pA[:, 0:TC], "pA", ct_h[:, cf, :], ct_l[:, cf, :],
                               ("ct_h", cf), ("ct_l", cf))

                k.dma("y", wb_h[:, :, :],
                      s_wkdec[0].rearrange("(a p) c -> p a c", p=128),
                      writes=["wb_h"])
                k.dma("y", wb_l[:, :, :],
                      s_wkdec[1].rearrange("(a p) c -> p a c", p=128),
                      writes=["wb_l"])
                for kv in range(KVH):
                    for cf in range(4):
                        lhh = wb_h[:, cf, kv * 128:(kv + 1) * 128]
                        lhl = wb_l[:, cf, kv * 128:(kv + 1) * 128]
                        k.mm(pA[:, 0:TC], lhh, ct_h[:, cf, :], ["wb_h"] + ct_reads,
                             "pA", cf == 0, False)
                        k.mm(pA[:, 0:TC], lhh, ct_l[:, cf, :], ["wb_h"] + ct_reads,
                             "pA", False, False)
                        k.mm(pA[:, 0:TC], lhl, ct_h[:, cf, :], ["wb_l"] + ct_reads,
                             "pA", False, cf == 3)
                    split_evac(pA[:, 0:TC], "pA",
                               kn_sb[0][:, kv, t0:t0 + TC],
                               kn_sb[1][:, kv, t0:t0 + TC],
                               ("kn", kv), ("knl", kv))

                k.dma("y", wb_h[:, :, :],
                      s_wvdec[0].rearrange("(a p) c -> p a c", p=128),
                      writes=["wb_h"])
                k.dma("y", wb_l[:, :, :],
                      s_wvdec[1].rearrange("(a p) c -> p a c", p=128),
                      writes=["wb_l"])
                for sub in range(2):
                    kidx = (t0 + sub * 128) // 128
                    for cf in range(4):
                        lhh = ct_h[:, cf, sub * 128:(sub + 1) * 128]
                        lhl = ct_l[:, cf, sub * 128:(sub + 1) * 128]
                        k.mm(pB[:, 0:512], lhh, wb_h[:, cf, :], ["wb_h"] + ct_reads,
                             "pB", cf == 0, False)
                        k.mm(pB[:, 0:512], lhl, wb_h[:, cf, :], ["wb_h"] + ct_reads,
                             "pB", False, False)
                        k.mm(pB[:, 0:512], lhh, wb_l[:, cf, :], ["wb_l"] + ct_reads,
                             "pB", False, cf == 3)
                    split_evac(pB[:, 0:512], "pB",
                               v_sb[0][:, kidx, :], v_sb[1][:, kidx, :],
                               ("v", kidx), ("vl", kidx))

                k.dma("y", wb_h[:, :, 0:256],
                      s_wkr[0].rearrange("(a p) c -> p a c", p=128),
                      writes=["wb_h"])
                k.dma("y", wb_l[:, :, 0:256],
                      s_wkr[1].rearrange("(a p) c -> p a c", p=128),
                      writes=["wb_l"])
                for blk in range(KVH // 2):
                    for cf in range(4):
                        lhh = wb_h[:, cf, blk * 128:(blk + 1) * 128]
                        lhl = wb_l[:, cf, blk * 128:(blk + 1) * 128]
                        k.mm(pA[:, 0:TC], lhh, ct_h[:, cf, :], ["wb_h"] + ct_reads,
                             "pA", cf == 0, False)
                        k.mm(pA[:, 0:TC], lhh, ct_l[:, cf, :], ["wb_h"] + ct_reads,
                             "pA", False, False)
                        k.mm(pA[:, 0:TC], lhl, ct_h[:, cf, :], ["wb_l"] + ct_reads,
                             "pA", False, cf == 3)
                    split_evac(pA[:, 0:TC], "pA",
                               kr_sb[0][:, blk, t0:t0 + TC],
                               kr_sb[1][:, blk, t0:t0 + TC],
                               ("kr", blk), ("krl", blk))

                for h in range(H):
                    if h % 4 == 0:
                        k.dma("y", wa_h[:, :, :],
                              s_wq[0][:, h * HD:(h + 4) * HD].rearrange(
                                  "(a p) c -> p a c", p=128), writes=["wa_h"])
                        k.dma("y", wa_l[:, :, :],
                              s_wq[1][:, h * HD:(h + 4) * HD].rearrange(
                                  "(a p) c -> p a c", p=128), writes=["wa_l"])
                    hc = (h % 4) * 128
                    for d in range(16):
                        lhh = wa_h[:, d, hc:hc + 128]
                        lhl = wa_l[:, d, hc:hc + 128]
                        k.mm(pA[:, 0:TC], lhh, xt_h[:, d, :], ["wa_h"] + xt_reads,
                             "pA", d == 0, False)
                        k.mm(pA[:, 0:TC], lhh, xt_l[:, d, :], ["wa_h"] + xt_reads,
                             "pA", False, False)
                        k.mm(pA[:, 0:TC], lhl, xt_h[:, d, :], ["wa_l"] + xt_reads,
                             "pA", False, d == 15)
                    split_evac(pA[:, 0:TC], "pA", ev_h[:, 0:TC], ev_l[:, 0:TC],
                               "ev_h", "ev_l")
                    k.dma("s", s_qn[0][h * 128:(h + 1) * 128, t0:t0 + TC],
                          ev_h[:, 0:TC], reads=["ev_h"], writes=["scr_qn0"])
                    k.dma("s", s_qn[1][h * 128:(h + 1) * 128, t0:t0 + TC],
                          ev_l[:, 0:TC], reads=["ev_l"], writes=["scr_qn1"])

                for blk in range(H // 2):
                    if blk % 4 == 0:
                        k.dma("y", wa_h[:, :, :],
                              s_wqr[0][:, blk * 128:(blk + 4) * 128].rearrange(
                                  "(a p) c -> p a c", p=128), writes=["wa_h"])
                        k.dma("y", wa_l[:, :, :],
                              s_wqr[1][:, blk * 128:(blk + 4) * 128].rearrange(
                                  "(a p) c -> p a c", p=128), writes=["wa_l"])
                    hc = (blk % 4) * 128
                    for d in range(16):
                        lhh = wa_h[:, d, hc:hc + 128]
                        lhl = wa_l[:, d, hc:hc + 128]
                        k.mm(pA[:, 0:TC], lhh, xt_h[:, d, :], ["wa_h"] + xt_reads,
                             "pA", d == 0, False)
                        k.mm(pA[:, 0:TC], lhh, xt_l[:, d, :], ["wa_h"] + xt_reads,
                             "pA", False, False)
                        k.mm(pA[:, 0:TC], lhl, xt_h[:, d, :], ["wa_l"] + xt_reads,
                             "pA", False, d == 15)
                    split_evac(pA[:, 0:TC], "pA", ev_h[:, 0:TC], ev_l[:, 0:TC],
                               "ev_h", "ev_l")
                    k.dma("s", s_qr[0][blk * 128:(blk + 1) * 128, t0:t0 + TC],
                          ev_h[:, 0:TC], reads=["ev_h"], writes=["scr_qr0"])
                    k.dma("s", s_qr[1][blk * 128:(blk + 1) * 128, t0:t0 + TC],
                          ev_l[:, 0:TC], reads=["ev_l"], writes=["scr_qr1"])
            ctx.close()

        phase_p0()
        k.new_phase()

        def phase_p1():
            ctx = contextlib.ExitStack()
            C = ctx.enter_context
            qn_h = C(nc.sbuf_tensor("qn_h", [128, S], BF16))
            qn_l = C(nc.sbuf_tensor("qn_l", [128, S], BF16))
            qr_h = C(nc.sbuf_tensor("qr_h", [128, S], BF16))
            qr_l = C(nc.sbuf_tensor("qr_l", [128, S], BF16))
            eP = C(nc.sbuf_tensor("eP", [128, 2, 512], F32))
            n_h = C(nc.sbuf_tensor("n_h", [128, 3, 512], BF16))
            n_l = C(nc.sbuf_tensor("n_l", [128, 3, 512], BF16))
            ones1 = C(nc.sbuf_tensor("ones1", [128, 1], BF16))
            zrow = C(nc.sbuf_tensor("zrow", [128, 512], F32))
            rbs = C(nc.sbuf_tensor("rbs", [128, 512], F32))
            otmp = C(nc.sbuf_tensor("otmp", [128, 512], F32))
            ot_hb = C(nc.sbuf_tensor("ot_hb", [128, 512], BF16))
            ot_lb = C(nc.sbuf_tensor("ot_lb", [128, 512], BF16))
            pS2 = C(nc.psum_tensor("pS2", [128, 1024], F32))
            pM = C(nc.psum_tensor("pM", [128, 512], F32))
            pO = C(nc.psum_tensor("pO", [128, 512], F32))
            pR = C(nc.psum_tensor("pR", [128, 512], F32))

            k.op("v", lambda e: e.memset(zrow[:], 0.0), writes=["zrow"])
            k.op("v", lambda e: e.memset(ones1[:, 0:1], 1.0), writes=["ones1"])

            NQT = S // 512
            for h in range(H):
                kv = h // (H // KVH)
                kvb = kv // 2
                kvs = (kv % 2) * 64
                k.dma("y", qn_h[:], s_qn[0][h * 128:(h + 1) * 128, :],
                      writes=["qn_h"])
                k.dma("y", qn_l[:], s_qn[1][h * 128:(h + 1) * 128, :],
                      writes=["qn_l"])
                qrb = (h // 8) * 4 + (h % 4)
                k.dma("y", qr_h[kvs:kvs + 64, :],
                      s_qr[0][qrb * 128 + kvs:qrb * 128 + kvs + 64, :],
                      writes=["qr_h"])
                k.dma("y", qr_l[kvs:kvs + 64, :],
                      s_qr[1][qrb * 128 + kvs:qrb * 128 + kvs + 64, :],
                      writes=["qr_l"])
                for qt in range(NQT):
                    q0 = qt * 512
                    nki = 4 * qt + 4
                    rq_nh = qn_h[:, q0:q0 + 512]
                    rq_nl = qn_l[:, q0:q0 + 512]
                    rq_rh = qr_h[kvs:kvs + 64, q0:q0 + 512]
                    rq_rl = qr_l[kvs:kvs + 64, q0:q0 + 512]
                    for ki in range(nki):
                        sbank = ki % 2
                        sap = pS2[:, sbank * 512:(sbank + 1) * 512]
                        sbk = ("pS2", sbank)
                        kn0 = kn_sb[0][:, kv, ki * 128:(ki + 1) * 128]
                        kn1 = kn_sb[1][:, kv, ki * 128:(ki + 1) * 128]
                        kr0 = kr_sb[0][kvs:kvs + 64, kvb, ki * 128:(ki + 1) * 128]
                        kr1 = kr_sb[1][kvs:kvs + 64, kvb, ki * 128:(ki + 1) * 128]
                        knb = ("kn", kv)
                        knlb = ("knl", kv)
                        krb = ("kr", kvb)
                        krlb = ("krl", kvb)
                        k.mm(sap, kn0, rq_nh, [knb, "qn_h"], sbk, True, False)
                        k.mm(sap, kn0, rq_nl, [knb, "qn_l"], sbk, False, False)
                        k.mm(sap, kn1, rq_nh, [knlb, "qn_h"], sbk, False, False)
                        k.mm(sap, kr0, rq_rh, [krb, "qr_h"], sbk, False, False)
                        k.mm(sap, kr0, rq_rl, [krb, "qr_l"], sbk, False, False)
                        k.mm(sap, kr1, rq_rh, [krlb, "qr_h"], sbk, False, True)
                        eslot = ki % 2
                        ev = eP[:, eslot, :]
                        ebk = ("eP", eslot)
                        k.op("s", lambda e, sap=sap, ev=ev: e.activation(
                            out=ev, in_=sap, func=ACT.Exp, scale=EXP_SCALE),
                            reads=[sbk], writes=[ebk])
                        j = ki - 4 * qt
                        if j >= 0:
                            k.op("v", lambda e, ev=ev, j=j: e.tensor_tensor(
                                out=ev, in0=ev, in1=dmask[:, j, :], op=ALU.mult),
                                reads=[ebk, "dmask"], writes=[ebk])
                        nslot = ki % 3
                        nhv = n_h[:, nslot, :]
                        nlv = n_l[:, nslot, :]
                        nhb = ("n_h", nslot)
                        nlb = ("n_l", nslot)
                        k.op("v", lambda e, ev=ev, nhv=nhv: e.tensor_copy(
                            out=nhv, in_=ev), reads=[ebk], writes=[nhb])
                        k.op("v", lambda e, ev=ev, nhv=nhv, nlv=nlv: e.tensor_tensor(
                            out=nlv, in0=ev, in1=nhv, op=ALU.subtract),
                            reads=[ebk, nhb], writes=[nlb])
                        k.mm(pM[0:1, 0:512], ones1[:, 0:1], nhv,
                             ["ones1", nhb], "pM", ki == 0, False)
                        k.mm(pM[0:1, 0:512], ones1[:, 0:1], nlv,
                             ["ones1", nlb], "pM", False, ki == nki - 1)
                        vh = v_sb[0][:, ki, kv * 128:(kv + 1) * 128]
                        vl = v_sb[1][:, ki, kv * 128:(kv + 1) * 128]
                        k.mm(pO[:, 0:512], vh, nhv, [("v", ki), nhb], "pO",
                             ki == 0, False)
                        k.mm(pO[:, 0:512], vl, nhv, [("vl", ki), nhb], "pO",
                             False, False)
                        k.mm(pO[:, 0:512], vh, nlv, [("v", ki), nlb], "pO",
                             False, ki == nki - 1, inc=True)
                    k.op("v", lambda e: e.reciprocal(
                        out=zrow[0:1, 0:512], in_=pM[0:1, 0:512]),
                        reads=["pM"], writes=["zrow"])
                    k.mm(pR[:, 0:512], ones[:], zrow[:, 0:512],
                         ["ones", "zrow"], "pR", True, True)
                    k.op("v", lambda e: e.tensor_copy(
                        out=rbs[:, 0:512], in_=pR[:, 0:512]),
                        reads=["pR"], writes=["rbs"])
                    k.op("v", lambda e: e.tensor_tensor(
                        out=otmp[:, 0:512], in0=pO[:, 0:512], in1=rbs[:, 0:512],
                        op=ALU.mult), reads=["pO", "rbs"], writes=["otmp"])
                    k.op("v", lambda e: e.tensor_copy(
                        out=ot_hb[:, 0:512], in_=otmp[:, 0:512]),
                        reads=["otmp"], writes=["ot_hb"])
                    k.op("v", lambda e: e.tensor_tensor(
                        out=ot_lb[:, 0:512], in0=otmp[:, 0:512],
                        in1=ot_hb[:, 0:512], op=ALU.subtract),
                        reads=["otmp", "ot_hb"], writes=["ot_lb"])
                    k.dma("s", s_ot[0][h * 128:(h + 1) * 128, q0:q0 + 512],
                          ot_hb[:, 0:512], reads=["ot_hb"], writes=["scr_ot0"])
                    k.dma("s", s_ot[1][h * 128:(h + 1) * 128, q0:q0 + 512],
                          ot_lb[:, 0:512], reads=["ot_lb"], writes=["scr_ot1"])
            ctx.close()

        phase_p1()
        ctx01.close()
        k.new_phase()

        xqT = E(nc.sbuf_tensor("xqT", [128, D // 128, S], BF16))
        s1inv = E(nc.sbuf_tensor("s1inv", [128, NSUB], F32))
        c2s = E(nc.sbuf_tensor("c2s", [128, NSUB], F32))
        is2 = E(nc.sbuf_tensor("is2", [128, NSUB], F32))
        xq_reads = [("xqT", dc) for dc in range(D // 128)]

        def phase_p2a():
            ctx = contextlib.ExitStack()
            C = ctx.enter_context
            xg2 = C(nc.sbuf_tensor("xg2", [128, D], F32))
            x2g = C(nc.sbuf_tensor("x2g", [128, D], F32))
            woh = C(nc.sbuf_tensor("woh", [128, D], BF16))
            wol = C(nc.sbuf_tensor("wol", [128, D], BF16))
            oth = C(nc.sbuf_tensor("oth", [128, 128], BF16))
            otl = C(nc.sbuf_tensor("otl", [128, 128], BF16))
            lnm = C(nc.sbuf_tensor("lnm", [128, 1], F32))
            lnv = C(nc.sbuf_tensor("lnv", [128, 1], F32))
            sqt = C(nc.sbuf_tensor("sqt", [128, D], F32))
            zq = C(nc.sbuf_tensor("zq", [128, D], F32))
            cq = C(nc.sbuf_tensor("cq", [128, D], F32))
            qiq = C(nc.sbuf_tensor("qiq", [128, D], BF16))
            pZ = C(nc.psum_tensor("pZ", [128, 2048], F32))
            pQT = C(nc.psum_tensor("pQT", [128, 512], F32))
            pQT2 = C(nc.psum_tensor("pQT2", [128, 512], F32))
            if True:
                for sub in range(NSUB):
                    t0 = sub * 128
                    k.dma("y", xg2[:, :], x_in[t0:t0 + 128, :], writes=["xg2"])
                    for h in range(H):
                        k.dma("y", woh[:, :], s_wo[0][h * 128:(h + 1) * 128, :],
                              writes=["woh"])
                        k.dma("y", wol[:, :], s_wo[1][h * 128:(h + 1) * 128, :],
                              writes=["wol"])
                        k.dma("y", oth[:, :],
                              s_ot[0][h * 128:(h + 1) * 128, t0:t0 + 128],
                              writes=["oth"])
                        k.dma("y", otl[:, :],
                              s_ot[1][h * 128:(h + 1) * 128, t0:t0 + 128],
                              writes=["otl"])
                        combos = [(oth, "oth", woh, "woh"),
                                  (oth, "oth", wol, "wol"),
                                  (otl, "otl", woh, "woh")]
                        for pi, (lh, lhb, wt, wtb) in enumerate(combos):
                            for dmc in range(4):
                                k.mm(pZ[:, dmc * 512:(dmc + 1) * 512], lh[:, :],
                                     wt[:, dmc * 512:(dmc + 1) * 512],
                                     [lhb, wtb], "pZ",
                                     start=(h == 0 and pi == 0),
                                     stop=(h == H - 1 and pi == 2),
                                     inc=(pi == 2 and dmc == 3))
                    k.op("v", lambda e: e.tensor_tensor(
                        out=x2g[:, :], in0=pZ[:, :], in1=xg2[:, :], op=ALU.add),
                        reads=["pZ", "xg2"], writes=["x2g"])
                    k.op("v", lambda e: e.reduce_sum(
                        out=lnm[:, 0:1], in_=x2g[:, :], axis=AX.X),
                        reads=["x2g"], writes=["lnm"])
                    k.op("v", lambda e: e.tensor_scalar(
                        out=lnm[:, 0:1], in0=lnm[:, 0:1], scalar1=1.0 / D,
                        scalar2=None, op0=ALU.mult), reads=["lnm"], writes=["lnm"])
                    k.op("v", lambda e: e.tensor_scalar(
                        out=x2g[:, :], in0=x2g[:, :], scalar1=lnm[:, 0:1],
                        scalar2=None, op0=ALU.subtract),
                        reads=["x2g", "lnm"], writes=["x2g"])
                    k.op("s", lambda e: e.square(out=sqt[:, :], in_=x2g[:, :]),
                         reads=["x2g"], writes=["sqt"])
                    k.op("v", lambda e: e.reduce_sum(
                        out=lnv[:, 0:1], in_=sqt[:, :], axis=AX.X),
                        reads=["sqt"], writes=["lnv"])
                    k.op("s", lambda e: e.activation(
                        out=lnv[:, 0:1], in_=lnv[:, 0:1], func=ACT.Sqrt,
                        bias=ceps[:, 0:1], scale=1.0 / D),
                        reads=["lnv", "ceps"], writes=["lnv"])
                    k.op("v", lambda e: e.reciprocal(
                        out=lnv[:, 0:1], in_=lnv[:, 0:1]),
                        reads=["lnv"], writes=["lnv"])
                    k.op("v", lambda e: e.tensor_scalar(
                        out=x2g[:, :], in0=x2g[:, :], scalar1=lnv[:, 0:1],
                        scalar2=None, op0=ALU.mult),
                        reads=["x2g", "lnv"], writes=["x2g"])
                    k.dma("s", s_x2[t0:t0 + 128, :], x2g[:, :], reads=["x2g"], writes=["scr_x2"])
                    k.op("v", lambda e: e.reduce_max(
                        out=lnm[:, 0:1], in_=x2g[:, :], axis=AX.X,
                        apply_absolute_value=True), reads=["x2g"], writes=["lnm"])
                    k.op("v", lambda e: e.tensor_scalar(
                        out=lnm[:, 0:1], in0=lnm[:, 0:1], scalar1=1e-5,
                        scalar2=None, op0=ALU.max), reads=["lnm"], writes=["lnm"])
                    k.op("v", lambda e, sub=sub: e.tensor_scalar(
                        out=s1inv[:, sub:sub + 1], in0=lnm[:, 0:1],
                        scalar1=1.0 / 127.0, scalar2=None, op0=ALU.mult),
                        reads=["lnm"], writes=["s1inv"])
                    k.op("v", lambda e: e.reciprocal(
                        out=lnv[:, 0:1], in_=lnm[:, 0:1]),
                        reads=["lnm"], writes=["lnv"])
                    k.op("v", lambda e: e.tensor_scalar(
                        out=lnv[:, 0:1], in0=lnv[:, 0:1], scalar1=127.0,
                        scalar2=None, op0=ALU.mult), reads=["lnv"], writes=["lnv"])
                    k.op("v", lambda e: e.tensor_scalar(
                        out=zq[:, :], in0=x2g[:, :], scalar1=lnv[:, 0:1],
                        scalar2=128.5, op0=ALU.mult, op1=ALU.add),
                        reads=["x2g", "lnv"], writes=["zq"])
                    for kb in range(7, -1, -1):
                        t = float(1 << kb)
                        k.op("v", lambda e, t=t: e.tensor_single_scalar(
                            out=cq[:, :], in_=zq[:, :], scalar=t, op=ALU.is_ge),
                            reads=["zq"], writes=["cq"])
                        k.op("v", lambda e, t=t: e.scalar_tensor_tensor(
                            out=zq[:, :], in0=cq[:, :], scalar=-t, in1=zq[:, :],
                            op0=ALU.mult, op1=ALU.add),
                            reads=["cq", "zq"], writes=["zq"])
                    k.op("v", lambda e: e.tensor_scalar(
                        out=cq[:, :], in0=x2g[:, :], scalar1=lnv[:, 0:1],
                        scalar2=0.5, op0=ALU.mult, op1=ALU.add),
                        reads=["x2g", "lnv"], writes=["cq"])
                    k.op("v", lambda e: e.tensor_tensor(
                        out=qiq[:, :], in0=cq[:, :], in1=zq[:, :],
                        op=ALU.subtract), reads=["cq", "zq"], writes=["qiq"])
                    for dc in range(D // 128):
                        bank, bkey = ((pQT, "pQT0") if dc % 2 == 0
                                      else (pQT2, "pQT1"))
                        k.mm(bank[:, 0:128], qiq[:, dc * 128:(dc + 1) * 128],
                             identb[:], ["qiq", "identb"], bkey, True, True)
                        k.op("v", lambda e, bank=bank, dc=dc, sub=sub:
                             e.tensor_copy(
                                 out=xqT[:, dc, sub * 128:(sub + 1) * 128],
                                 in_=bank[:, 0:128]),
                             reads=[bkey], writes=[("xqT", dc)])
            ctx.close()

        if upto >= 3:
            phase_p2a()
        k.new_phase()

        def phase_p2b():
            ctx = contextlib.ExitStack()
            C = ctx.enter_context
            w13f = C(nc.sbuf_tensor("w13f", [128, 16, 512], F32))
            w13h = C(nc.sbuf_tensor("w13h", [128, 16, 512], BF16))
            w13l = C(nc.sbuf_tensor("w13l", [128, 16, 512], BF16))
            msac = C(nc.sbuf_tensor("msac", [128, NSUB], F32))
            mxac = C(nc.sbuf_tensor("mxac", [128, NSUB], F32))
            rg1 = C(nc.sbuf_tensor("rg1", [128, 256], F32))
            rg2 = C(nc.sbuf_tensor("rg2", [128, 256], F32))
            rg3 = C(nc.sbuf_tensor("rg3", [128, 256], F32))
            red1 = C(nc.sbuf_tensor("red1", [128, 1], F32))
            nwbs = C(nc.sbuf_tensor("nwbs", [128, FH], F32))
            pF = C(nc.psum_tensor("pF", [128, 1024], F32))
            if True:
                w13ff = w13f.rearrange("p a b -> p (a b)")
                w13hf = w13h.rearrange("p a b -> p (a b)")
                w13lf = w13l.rearrange("p a b -> p (a b)")
                k.dma("y", nwbs[:], nwb_in, writes=["nwbs"])
                k.op("v", lambda e: e.memset(msac[:], 0.0), writes=["msac"])
                k.op("v", lambda e: e.memset(mxac[:], 0.0), writes=["mxac"])

                for fc in range(NFC):
                    k.dma("y", w13f[:, :, :],
                          w13t_in[:, fc * 512:(fc + 1) * 512].rearrange(
                              "(a p) c -> p a c", p=128), writes=["w13f"])
                    k.op("v", lambda e: e.tensor_copy(out=w13hf, in_=w13ff),
                         reads=["w13f"], writes=["w13h"])
                    k.op("v", lambda e: e.tensor_tensor(
                        out=w13lf, in0=w13ff, in1=w13hf, op=ALU.subtract),
                        reads=["w13f", "w13h"], writes=["w13l"])
                    if P2BCUT < 1:
                        continue
                    for sub in range(NSUB):
                        bank = sub % 2
                        for pi, (wt, wtb) in enumerate([(w13h, "w13h"),
                                                        (w13l, "w13l")]):
                            for d in range(16):
                                k.mm(pF[:, bank * 512:bank * 512 + 512],
                                     xqT[:, d, sub * 128:(sub + 1) * 128],
                                     wt[:, d, :], xq_reads + [wtb], ("pF", bank),
                                     start=(pi == 0 and d == 0),
                                     stop=(pi == 1 and d == 15))
                        if P2BCUT < 2:
                            continue
                        x1p = pF[:, bank * 512:bank * 512 + 256]
                        x3p = pF[:, bank * 512 + 256:bank * 512 + 512]
                        k.op("s", lambda e, x1p=x1p, sub=sub: e.activation(
                            out=rg1[:, :], in_=x1p, func=ACT.Relu,
                            scale=s1inv[:, sub:sub + 1]),
                            reads=[("pF", bank), "s1inv"], writes=["rg1"])
                        if P2BCUT < 21:
                            continue
                        k.op("s", lambda e: e.square(out=rg2[:, :], in_=rg1[:, :]),
                             reads=["rg1"], writes=["rg2"])
                        if P2BCUT < 22:
                            continue
                        k.op("s", lambda e, x3p=x3p: e.copy(
                            out=rg3[:, :], in_=x3p),
                            reads=[("pF", bank)], writes=["rg3"])
                        if P2BCUT < 23:
                            continue
                        k.op("v", lambda e, sub=sub: e.scalar_tensor_tensor(
                            out=rg1[:, :], in0=rg2[:, :],
                            scalar=s1inv[:, sub:sub + 1], in1=rg3[:, :],
                            op0=ALU.mult, op1=ALU.mult),
                            reads=["rg2", "rg3", "s1inv"], writes=["rg1"])
                        if P2BCUT < 24:
                            continue

                        k.op("s", lambda e: e.square(out=rg2[:, :], in_=rg1[:, :]),
                             reads=["rg1"], writes=["rg2"])
                        k.op("v", lambda e: e.reduce_sum(
                            out=red1[:, 0:1], in_=rg2[:, :], axis=AX.X),
                            reads=["rg2"], writes=["red1"])
                        k.op("v", lambda e, sub=sub: e.tensor_tensor(
                            out=msac[:, sub:sub + 1], in0=msac[:, sub:sub + 1],
                            in1=red1[:, 0:1], op=ALU.add),
                            reads=["msac", "red1"], writes=["msac"])
                        k.op("v", lambda e, fc=fc: e.tensor_tensor(
                            out=rg3[:, :], in0=rg1[:, :],
                            in1=nwbs[:, fc * 256:(fc + 1) * 256], op=ALU.mult),
                            reads=["rg1", "nwbs"], writes=["rg3"])
                        k.op("v", lambda e: e.reduce_max(
                            out=red1[:, 0:1], in_=rg3[:, :], axis=AX.X,
                            apply_absolute_value=True),
                            reads=["rg3"], writes=["red1"])
                        k.op("v", lambda e, sub=sub: e.tensor_tensor(
                            out=mxac[:, sub:sub + 1], in0=mxac[:, sub:sub + 1],
                            in1=red1[:, 0:1], op=ALU.max),
                            reads=["mxac", "red1"], writes=["mxac"])
                        if P2BCUT >= 4:
                            k.dma("s", s_gnw[sub * 128:(sub + 1) * 128,
                                            fc * 256:(fc + 1) * 256],
                                  rg3[:, :], reads=["rg3"], writes=["scr_gnw"])

                for sub in range(NSUB):
                    k.op("s", lambda e, sub=sub: e.activation(
                        out=c2s[:, sub:sub + 1], in_=msac[:, sub:sub + 1],
                        func=ACT.Sqrt, bias=ceps[:, 0:1], scale=1.0 / FH),
                        reads=["msac", "ceps"], writes=[("c2s", sub)])
                    k.op("v", lambda e, sub=sub: e.reciprocal(
                        out=c2s[:, sub:sub + 1], in_=c2s[:, sub:sub + 1]),
                        reads=[("c2s", sub)], writes=[("c2s", sub)])
                    k.op("v", lambda e, sub=sub: e.tensor_tensor(
                        out=is2[:, sub:sub + 1], in0=mxac[:, sub:sub + 1],
                        in1=c2s[:, sub:sub + 1], op=ALU.mult),
                        reads=["mxac", ("c2s", sub)], writes=[("is2", sub)])
                    k.op("v", lambda e, sub=sub: e.tensor_scalar(
                        out=is2[:, sub:sub + 1], in0=is2[:, sub:sub + 1],
                        scalar1=1e-5, scalar2=None, op0=ALU.max),
                        reads=[("is2", sub)], writes=[("is2", sub)])
                    k.op("v", lambda e, sub=sub: e.reciprocal(
                        out=red1[:, 0:1], in_=is2[:, sub:sub + 1]),
                        reads=[("is2", sub)], writes=["red1"])
                    k.op("v", lambda e: e.tensor_scalar(
                        out=red1[:, 0:1], in0=red1[:, 0:1], scalar1=127.0,
                        scalar2=None, op0=ALU.mult), reads=["red1"], writes=["red1"])
                    k.op("v", lambda e, sub=sub: e.tensor_tensor(
                        out=c2s[:, sub:sub + 1], in0=red1[:, 0:1],
                        in1=c2s[:, sub:sub + 1], op=ALU.mult),
                        reads=["red1", ("c2s", sub)], writes=[("c2s", sub)])
                    k.op("v", lambda e, sub=sub: e.tensor_scalar(
                        out=is2[:, sub:sub + 1], in0=is2[:, sub:sub + 1],
                        scalar1=1.0 / 127.0, scalar2=None, op0=ALU.mult),
                        reads=[("is2", sub)], writes=[("is2", sub)])
            ctx.close()

        if upto >= 4:
            phase_p2b()
        k.new_phase()

        def phase_p2c():
            ctx = contextlib.ExitStack()
            C = ctx.enter_context
            gbuf = C(nc.sbuf_tensor("gbuf", [128, 512], F32))
            zq2 = C(nc.sbuf_tensor("zq2", [128, 512], F32))
            cq2 = C(nc.sbuf_tensor("cq2", [128, 512], F32))
            qi2 = C(nc.sbuf_tensor("qi2", [128, 512], BF16))
            iqT = C(nc.sbuf_tensor("iqT", [128, NFH, 512], BF16))
            w2f = C(nc.sbuf_tensor("w2f", [128, 512], F32))
            w2h = C(nc.sbuf_tensor("w2h", [128, 512], BF16))
            w2l = C(nc.sbuf_tensor("w2l", [128, 512], BF16))
            x2b = C(nc.sbuf_tensor("x2b", [128, 512], F32))
            y2 = C(nc.sbuf_tensor("y2", [128, 16, 512], F32))
            lnm2 = C(nc.sbuf_tensor("lnm2", [128, 1], F32))
            lnv2 = C(nc.sbuf_tensor("lnv2", [128, 1], F32))
            sqt2 = C(nc.sbuf_tensor("sqt2", [128, D], F32))
            pG = C(nc.psum_tensor("pG", [128, 2048], F32))
            pQ2 = C(nc.psum_tensor("pQ2", [128, 512], F32))
            pQ22 = C(nc.psum_tensor("pQ22", [128, 512], F32))
            if True:
                for sg in range(NSUB // 4):
                    subs = [sg * 4 + i for i in range(4)]
                    for si, sub in enumerate(subs):
                        for f0 in range(0, FH, 512):
                            k.dma("y", gbuf[:, :],
                                  s_gnw[sub * 128:(sub + 1) * 128, f0:f0 + 512],
                                  writes=["gbuf"])
                            k.op("v", lambda e, sub=sub: e.tensor_scalar(
                                out=zq2[:, :], in0=gbuf[:, :],
                                scalar1=c2s[:, sub:sub + 1], scalar2=128.5,
                                op0=ALU.mult, op1=ALU.add),
                                reads=["gbuf", ("c2s", sub)], writes=["zq2"])
                            for kb in range(7, -1, -1):
                                t = float(1 << kb)
                                k.op("v", lambda e, t=t: e.tensor_single_scalar(
                                    out=cq2[:, :], in_=zq2[:, :], scalar=t,
                                    op=ALU.is_ge), reads=["zq2"], writes=["cq2"])
                                k.op("v", lambda e, t=t: e.scalar_tensor_tensor(
                                    out=zq2[:, :], in0=cq2[:, :], scalar=-t,
                                    in1=zq2[:, :], op0=ALU.mult, op1=ALU.add),
                                    reads=["cq2", "zq2"], writes=["zq2"])
                            k.op("v", lambda e, sub=sub: e.tensor_scalar(
                                out=cq2[:, :], in0=gbuf[:, :],
                                scalar1=c2s[:, sub:sub + 1], scalar2=0.5,
                                op0=ALU.mult, op1=ALU.add),
                                reads=["gbuf", ("c2s", sub)], writes=["cq2"])
                            k.op("v", lambda e: e.tensor_tensor(
                                out=qi2[:, :], in0=cq2[:, :], in1=zq2[:, :],
                                op=ALU.subtract), reads=["cq2", "zq2"],
                                writes=["qi2"])
                            for j in range(4):
                                fch = f0 // 128 + j
                                bank, bkey = ((pQ2, "pQ20") if j % 2 == 0
                                              else (pQ22, "pQ21"))
                                k.mm(bank[:, 0:128],
                                     qi2[:, j * 128:(j + 1) * 128],
                                     identb[:], ["qi2", "identb"], bkey,
                                     True, True)
                                k.op("v", lambda e, bank=bank, j=j, fch=fch,
                                     si=si: e.tensor_copy(
                                         out=iqT[:, fch, si * 128:(si + 1) * 128],
                                         in_=bank[:, 0:128]),
                                     reads=[bkey], writes=[("iqT", fch)])
                    iq_reads = [("iqT", f) for f in range(NFH)]
                    for dmc in range(4):
                        for fch in range(NFH):
                            k.dma("y", w2f[:, :],
                                  w2t_in[fch * 128:(fch + 1) * 128,
                                         dmc * 512:(dmc + 1) * 512],
                                  writes=["w2f"])
                            k.op("v", lambda e: e.tensor_copy(
                                out=w2h[:, :], in_=w2f[:, :]),
                                reads=["w2f"], writes=["w2h"])
                            k.op("v", lambda e: e.tensor_tensor(
                                out=w2l[:, :], in0=w2f[:, :], in1=w2h[:, :],
                                op=ALU.subtract), reads=["w2f", "w2h"],
                                writes=["w2l"])
                            for si in range(4):
                                k.mm(pG[:, si * 512:(si + 1) * 512],
                                     iqT[:, fch, si * 128:(si + 1) * 128],
                                     w2h[:, :], iq_reads + ["w2h"], ("pG", si),
                                     start=(fch == 0), stop=False)
                                k.mm(pG[:, si * 512:(si + 1) * 512],
                                     iqT[:, fch, si * 128:(si + 1) * 128],
                                     w2l[:, :], iq_reads + ["w2l"], ("pG", si),
                                     False, stop=(fch == NFH - 1),
                                     inc=(si == 3))
                        for si, sub in enumerate(subs):
                            k.dma("y", x2b[:, :],
                                  s_x2[sub * 128:(sub + 1) * 128,
                                       dmc * 512:(dmc + 1) * 512],
                                  writes=["x2b"])
                            k.op("v", lambda e, si=si, sub=sub, dmc=dmc:
                                 e.scalar_tensor_tensor(
                                     out=y2[:, si * 4 + dmc, :],
                                     in0=pG[:, si * 512:(si + 1) * 512],
                                     scalar=is2[:, sub:sub + 1], in1=x2b[:, :],
                                     op0=ALU.mult, op1=ALU.add),
                                 reads=[("pG", si), ("is2", sub), "x2b"],
                                 writes=[("y2", si, dmc)])
                    for si, sub in enumerate(subs):
                        yv = y2[:, si * 4:(si + 1) * 4, :].rearrange(
                            "p a b -> p (a b)")
                        yr = [("y2", si, d) for d in range(4)]
                        k.op("v", lambda e, yv=yv: e.reduce_sum(
                            out=lnm2[:, 0:1], in_=yv, axis=AX.X),
                            reads=yr, writes=["lnm2"])
                        k.op("v", lambda e: e.tensor_scalar(
                            out=lnm2[:, 0:1], in0=lnm2[:, 0:1], scalar1=1.0 / D,
                            scalar2=None, op0=ALU.mult),
                            reads=["lnm2"], writes=["lnm2"])
                        k.op("v", lambda e, yv=yv: e.tensor_scalar(
                            out=yv, in0=yv, scalar1=lnm2[:, 0:1], scalar2=None,
                            op0=ALU.subtract), reads=yr + ["lnm2"], writes=yr)
                        k.op("s", lambda e, yv=yv: e.square(
                            out=sqt2[:, :], in_=yv), reads=yr, writes=["sqt2"])
                        k.op("v", lambda e: e.reduce_sum(
                            out=lnv2[:, 0:1], in_=sqt2[:, :], axis=AX.X),
                            reads=["sqt2"], writes=["lnv2"])
                        k.op("s", lambda e: e.activation(
                            out=lnv2[:, 0:1], in_=lnv2[:, 0:1], func=ACT.Sqrt,
                            bias=ceps[:, 0:1], scale=1.0 / D),
                            reads=["lnv2", "ceps"], writes=["lnv2"])
                        k.op("v", lambda e: e.reciprocal(
                            out=lnv2[:, 0:1], in_=lnv2[:, 0:1]),
                            reads=["lnv2"], writes=["lnv2"])
                        k.op("v", lambda e, yv=yv: e.tensor_scalar(
                            out=yv, in0=yv, scalar1=lnv2[:, 0:1], scalar2=None,
                            op0=ALU.mult), reads=yr + ["lnv2"], writes=yr)
                        k.dma("s", out_t[sub * 128:(sub + 1) * 128, :], yv,
                              reads=yr, writes=["scr_out"])
            ctx.close()

        if upto >= 5:
            phase_p2c()
        k.wait_all_on("y")

    return nc


def _rope_factor_tables(pairs):
    theta = 10000.0
    freqs = (1.0 / theta ** (np.arange(0, RD, 2, dtype=np.float32)
                             / np.float32(RD))).astype(np.float32)
    t = np.arange(4096, dtype=np.float32)
    f = np.outer(t, freqs).astype(np.float32)
    cos = np.cos(f).astype(np.float32)
    sin = np.sin(f).astype(np.float32)
    m1 = []
    m2 = []
    for ha, hb in pairs:
        for h in (ha, hb):
            m1.append(cos[h]); m1.append(cos[h])
            m2.append(-sin[h]); m2.append(sin[h])
    m1 = np.concatenate(m1).astype(np.float32)[None, :]
    m2 = np.concatenate(m2).astype(np.float32)[None, :]
    return (np.ascontiguousarray(np.broadcast_to(m1, (128, m1.shape[1]))),
            np.ascontiguousarray(np.broadcast_to(m2, (128, m2.shape[1]))))


def _perm_rope_cols(n_heads, pairs=None):
    if pairs is None:
        pairs = [(2 * b, 2 * b + 1) for b in range(n_heads // 2)]
    perm = []
    for ha, hb in pairs:
        for h in (ha, hb):
            perm.extend((h * RD + 2 * np.arange(32)).tolist())
            perm.extend((h * RD + 2 * np.arange(32) + 1).tolist())
    return np.asarray(perm)


_Q_PAIRS = [(0, 4), (1, 5), (2, 6), (3, 7), (8, 12), (9, 13), (10, 14), (11, 15)]


def _diag_masks():
    k = np.arange(128)[:, None]
    q = np.arange(512)[None, :]
    return np.ascontiguousarray(np.concatenate(
        [(q >= k + 128 * j).astype(np.float32) for j in range(4)], axis=0))


def _prep_shared(Wq, Wkvc, Wkdec, Wvdec, Wqrope, Wkrope, Wo, w13, w2, norm_w, S):
    f32 = np.float32
    wq = np.ascontiguousarray(np.asarray(Wq, dtype=f32))
    wqr = np.ascontiguousarray(np.asarray(Wqrope, dtype=f32)[:, _perm_rope_cols(H, _Q_PAIRS)])
    wkvc = np.ascontiguousarray(np.asarray(Wkvc, dtype=f32))
    wkdec = np.ascontiguousarray(np.asarray(Wkdec, dtype=f32))
    wvdec = np.ascontiguousarray(np.asarray(Wvdec, dtype=f32))
    wkr = np.ascontiguousarray(np.asarray(Wkrope, dtype=f32)[:, _perm_rope_cols(KVH)])
    wo_full = np.asarray(Wo, dtype=f32)
    rows = np.concatenate([np.arange(h * (HD + RD), h * (HD + RD) + HD)
                           for h in range(H)])
    wo = np.ascontiguousarray(wo_full[rows, :])
    w13t = np.asarray(w13, dtype=f32).T
    blocks = []
    for j in range(FH // 256):
        blocks.append(w13t[:, j * 256:(j + 1) * 256])
        blocks.append(w13t[:, FH + j * 256:FH + (j + 1) * 256])
    w13t_i = np.ascontiguousarray(np.concatenate(blocks, axis=1))
    w2t = np.ascontiguousarray(np.asarray(w2, dtype=f32).T)
    nwb = np.ascontiguousarray(np.broadcast_to(
        np.asarray(norm_w, dtype=f32)[None, :], (128, FH)))
    m1q, m2q = _rope_factor_tables(_Q_PAIRS)
    m1k, m2k = _rope_factor_tables([(0, 1), (2, 3)])
    return dict(wq=wq, wqr=wqr, wkvc=wkvc, wkdec=wkdec, wvdec=wvdec, wkr=wkr,
                wo=wo, w13t=w13t_i, w2t=w2t, nwb=nwb,
                ident=np.eye(128, dtype=f32),
                identb=np.eye(128, dtype=ml_dtypes.bfloat16),
                dmask=_diag_masks(),
                ones=np.ones((128, 128), dtype=f32),
                m1q=m1q, m2q=m2q, m1k=m1k, m2k=m2k)


def kernel(L0_attn_hidden_states, L0_attn_Wq, L0_attn_Wkvc, L0_attn_Wkdec,
           L0_attn_Wvdec, L0_attn_Wqrope, L0_attn_Wkrope, L0_attn_Wo,
           L0_ffn_w13, L0_ffn_w2, L0_ffn_norm_w):
    x = np.ascontiguousarray(np.asarray(L0_attn_hidden_states, dtype=np.float32))
    B, S, _ = x.shape
    shared = _prep_shared(L0_attn_Wq, L0_attn_Wkvc, L0_attn_Wkdec, L0_attn_Wvdec,
                          L0_attn_Wqrope, L0_attn_Wkrope, L0_attn_Wo,
                          L0_ffn_w13, L0_ffn_w2, L0_ffn_norm_w, S)
    nc = build_core_program(S)
    in_maps = []
    for b in range(B):
        m = dict(shared)
        m["x"] = np.ascontiguousarray(x[b])
        in_maps.append(m)
    res = run_bass_kernel_spmd(nc, in_maps, core_ids=list(range(B)))
    return np.stack([np.asarray(res.results[b]["out"]) for b in range(B)], axis=0)


# version 07
# speedup vs baseline: 1.3300x; 1.3300x over previous
import contextlib
import math

import numpy as np
import ml_dtypes

import concourse.bass as bass
import concourse.mybir as mybir
from concourse.bass_utils import run_bass_kernel_spmd

F32 = mybir.dt.float32
BF16 = mybir.dt.bfloat16
ALU = mybir.AluOpType
ACT = mybir.ActivationFunctionType
AX = mybir.AxisListType

D = 2048
H = 16
KVH = 4
HD = 128
RD = 64
CF = 512
FH = 5632
LN_EPS = 1e-5
P1H = 99
P1QC = 99
P1CUT = 99
P2BCUT = 99
EXP_SCALE = float(np.float32(1.0) / np.sqrt(np.float32(HD + RD)))


class KB:
    ENG = ("t", "v", "s", "g", "y")

    def __init__(self, nc, st):
        self.nc = nc
        self.st = st
        self.eng = {"t": nc.tensor, "v": nc.vector, "s": nc.scalar,
                    "g": nc.gpsimd, "y": nc.sync}
        self.sems = {}
        self.cnt = {}
        self.waited = {e: {} for e in self.ENG}
        self.state = {}
        self.pending_reads = []
        self.dma_slots = {}
        self.phase = 0
        self.barrier_snapshot = None
        self.barrier_done = set()

    def _sem(self, key):
        if key not in self.sems:
            self.sems[key] = self.st.enter_context(
                self.nc.semaphore(f"sm{key[0]}{key[1]}{key[2]}"))
            self.cnt[key] = 0
        return self.sems[key]

    def new_phase(self):
        self.barrier_snapshot = {k: v for k, v in self.cnt.items() if v > 0}
        self.phase += 1
        self.barrier_done = set()
        self.state = {}

    def _apply_barrier(self, e):
        if self.barrier_snapshot is None or e in self.barrier_done:
            return
        self.barrier_done.add(e)
        for kk, v in self.barrier_snapshot.items():
            self._wait(e, kk, v)

    def _wait(self, e, semkey, val):
        if val <= 0:
            return
        w = self.waited[e]
        if w.get(semkey, 0) >= val:
            return
        w[semkey] = val
        self.eng[e].wait_ge(self._sem(semkey), val)

    def _dep_waits(self, e, reads, writes):
        self._apply_barrier(e)
        need = {}
        for b in reads:
            s = self.state.get(b)
            if s and s["w"]:
                kk, v = s["w"]
                need[kk] = max(need.get(kk, 0), v)
        for b in writes:
            s = self.state.get(b)
            if s:
                if s["w"]:
                    kk, v = s["w"]
                    need[kk] = max(need.get(kk, 0), v)
                for kk, v in s["r"].items():
                    need[kk] = max(need.get(kk, 0), v)
        for kk, v in need.items():
            self._wait(e, kk, v)

    def _note(self, semkey, val, reads, writes):
        for b in writes:
            self.state[b] = {"w": (semkey, val), "r": {}}
        for b in reads:
            s = self.state.setdefault(b, {"w": None, "r": {}})
            s["r"][semkey] = max(s["r"].get(semkey, 0), val)

    def op(self, e, fn, reads=(), writes=()):
        reads, writes = list(reads), list(writes)
        self._dep_waits(e, reads, writes)
        instr = fn(self.eng[e])
        kk = ("e", e, self.phase)
        self._sem(kk)
        self.cnt[kk] += 1
        instr.then_inc(self.sems[kk], 1)
        self._note(kk, self.cnt[kk], reads, writes)
        return instr

    def mm(self, psum_ap, lhsT_ap, rhs_ap, reads, psum_buf, start, stop,
           inc=False):
        reads = list(reads)
        self._dep_waits("t", reads, [psum_buf] if start else [])
        instr = self.eng["t"].matmul(psum_ap, lhsT_ap, rhs_ap,
                                     start=start, stop=stop)
        kk = ("e", "t", self.phase)
        self._sem(kk)
        if stop or inc:
            self.cnt[kk] += 1
            instr.then_inc(self.sems[kk], 1)
            self._note(kk, self.cnt[kk], reads,
                       [psum_buf] if stop else [])
        else:
            self._note(kk, self.cnt[kk] + 1, reads, [])
        return instr

    def transpose(self, psum_ap, in_ap, ident_ap, reads, psum_buf):
        reads = list(reads)
        self._dep_waits("t", reads, [psum_buf])
        instr = self.eng["t"].transpose(psum_ap, in_ap, ident_ap)
        kk = ("e", "t", self.phase)
        self._sem(kk)
        self.cnt[kk] += 1
        instr.then_inc(self.sems[kk], 1)
        self._note(kk, self.cnt[kk], reads, [psum_buf])
        return instr

    def dma(self, q, out_ap, in_ap, reads=(), writes=()):
        reads, writes = list(reads), list(writes)
        assert writes, "every DMA must declare a destination buffer key"
        self._dep_waits(q, reads, writes)
        instr = self.eng[q].dma_start(out=out_ap, in_=in_ap)
        slot = self.dma_slots.setdefault((self.phase, writes[0]),
                                         len(self.dma_slots))
        kk = ("d", slot, self.phase)
        self._sem(kk)
        self.cnt[kk] += 16
        instr.then_inc(self.sems[kk], 16)
        self._note(kk, self.cnt[kk], reads, writes)
        return instr

    def wait_all_on(self, e):
        for kk, v in list(self.cnt.items()):
            if v > 0:
                self._wait(e, kk, v)


def build_core_program(S, upto=99):
    TC = 512
    NTC = S // TC
    NQC = S // 128
    NSUB = S // 128
    NFC = 2 * FH // 512
    NFH = FH // 128
    nc = bass.Bass("TRN2", target_bir_lowering=False, debug=False)

    def din(name, shape, dt=F32):
        return nc.dram_tensor(name, list(shape), dt, kind="ExternalInput").ap()

    x_in = din("x", (S, D))
    wq_in = din("wq", (D, H * HD))
    wqr_in = din("wqr", (D, H * RD))
    wkvc_in = din("wkvc", (D, CF))
    wkdec_in = din("wkdec", (CF, KVH * HD))
    wvdec_in = din("wvdec", (CF, KVH * HD))
    wkr_in = din("wkr", (CF, KVH * RD))
    wo_in = din("wo", (H * HD, D))
    w13t_in = din("w13t", (D, 2 * FH))
    w2t_in = din("w2t", (FH, D))
    nwb_in = din("nwb", (128, FH))
    ident_in = din("ident", (128, 128))
    identb_in = din("identb", (128, 128), BF16)
    dmask_in = din("dmask", (512, 512))
    ones_in = din("ones", (128, 128))
    m1q_in = din("m1q", (128, H * RD))
    m2q_in = din("m2q", (128, H * RD))
    m1k_in = din("m1k", (128, KVH * RD))
    m2k_in = din("m2k", (128, KVH * RD))
    out_t = nc.dram_tensor("out", [S, D], F32, kind="ExternalOutput").ap()

    def scr(name, shape, dt=BF16):
        return nc.dram_tensor(name, list(shape), dt, kind="Internal").ap()

    s_wq = [scr("s_wq_h", (D, H * HD)), scr("s_wq_l", (D, H * HD))]
    s_wqr = [scr("s_wqr_h", (D, H * RD)), scr("s_wqr_l", (D, H * RD))]
    s_wkvc = [scr("s_wkvc_h", (D, CF)), scr("s_wkvc_l", (D, CF))]
    s_wkdec = [scr("s_wkdec_h", (CF, KVH * HD)), scr("s_wkdec_l", (CF, KVH * HD))]
    s_wvdec = [scr("s_wvdec_h", (CF, KVH * HD)), scr("s_wvdec_l", (CF, KVH * HD))]
    s_wkr = [scr("s_wkr_h", (CF, KVH * RD)), scr("s_wkr_l", (CF, KVH * RD))]
    s_wo = [scr("s_wo_h", (H * HD, D)), scr("s_wo_l", (H * HD, D))]
    s_qn = [scr("s_qn_h", (H * HD, S)), scr("s_qn_l", (H * HD, S))]
    s_qr = [scr("s_qr_h", (H * RD, S)), scr("s_qr_l", (H * RD, S))]
    s_ot = [scr("s_ot_h", (H * HD, S)), scr("s_ot_l", (H * HD, S))]
    s_x2 = scr("s_x2", (S, D), F32)
    s_gnw = scr("s_gnw", (S, FH), F32)

    st = contextlib.ExitStack()
    with st:
        E = st.enter_context
        k = KB(nc, st)

        identb = E(nc.sbuf_tensor("sb_identb", [128, 128], BF16))

        ones = E(nc.sbuf_tensor("sb_ones", [128, 128], F32))
        c127 = E(nc.sbuf_tensor("c127", [128, 1], F32))
        ceps = E(nc.sbuf_tensor("ceps", [128, 1], F32))
        ctx01 = contextlib.ExitStack()
        kn_sb = [ctx01.enter_context(nc.sbuf_tensor(f"kn_{p}", [128, KVH, S], BF16)) for p in "hl"]
        kr_sb = [ctx01.enter_context(nc.sbuf_tensor(f"kr_{p}", [128, KVH // 2, S], BF16)) for p in "hl"]
        v_sb = [ctx01.enter_context(nc.sbuf_tensor(f"v_{p}", [128, S // 128, KVH * HD], BF16)) for p in "hl"]

        k.dma("y", identb[:], identb_in, writes=["identb"])

        k.dma("y", ones[:], ones_in, writes=["ones"])
        k.op("v", lambda e: e.memset(c127[:, 0:1], 127.0), writes=["c127"])
        k.op("v", lambda e: e.memset(ceps[:, 0:1], LN_EPS), writes=["ceps"])

        def split_evac(psum_ap, psum_buf, hi_ap, lo_ap, hi_buf, lo_buf):
            k.op("v", lambda e: e.tensor_copy(out=hi_ap, in_=psum_ap),
                 reads=[psum_buf], writes=[hi_buf])
            k.op("v", lambda e: e.tensor_tensor(
                out=lo_ap, in0=psum_ap, in1=hi_ap, op=ALU.subtract),
                reads=[psum_buf, hi_buf], writes=[lo_buf])

        def phase_pre():
            ctx = ctx01
            C = ctx.enter_context
            spl_f = C(nc.sbuf_tensor("spl_f", [128, 1024], F32))
            spl_h = C(nc.sbuf_tensor("spl_h", [128, 1024], BF16))
            spl_l = C(nc.sbuf_tensor("spl_l", [128, 1024], BF16))
            spl_t = C(nc.sbuf_tensor("spl_t", [128, 1024], F32))
            spl_u = C(nc.sbuf_tensor("spl_u", [128, 1024], F32))
            m1q_sb = C(nc.sbuf_tensor("m1q_sb", [128, H * RD], F32))
            m2q_sb = C(nc.sbuf_tensor("m2q_sb", [128, H * RD], F32))
            m1k_sb = C(nc.sbuf_tensor("m1k_sb", [128, KVH * RD], F32))
            m2k_sb = C(nc.sbuf_tensor("m2k_sb", [128, KVH * RD], F32))
            k.dma("y", m1q_sb[:], m1q_in, writes=["m1q"])
            k.dma("y", m2q_sb[:], m2q_in, writes=["m2q"])
            k.dma("y", m1k_sb[:], m1k_in, writes=["m1k"])
            k.dma("y", m2k_sb[:], m2k_in, writes=["m2k"])
            specs = [(wq_in, s_wq, D, H * HD, None),
                     (wqr_in, s_wqr, D, H * RD, (m1q_sb, m2q_sb, "m1q", "m2q")),
                     (wkvc_in, s_wkvc, D, CF, None),
                     (wkdec_in, s_wkdec, CF, KVH * HD, None),
                     (wvdec_in, s_wvdec, CF, KVH * HD, None),
                     (wkr_in, s_wkr, CF, KVH * RD, (m1k_sb, m2k_sb, "m1k", "m2k")),
                     (wo_in, s_wo, H * HD, D, None)]
            for w_in, w_scr, R, C, ropem in specs:
                for r0 in range(0, R, 128):
                    for c0 in range(0, C, 1024):
                        cw = min(1024, C - c0)
                        k.dma("y", spl_f[:, 0:cw], w_in[r0:r0 + 128, c0:c0 + cw],
                              writes=["spl_f"])
                        if ropem is not None:
                            m1t, m2t, m1b, m2b = ropem
                            nb = cw // 64
                            w4 = spl_f[:, 0:cw].rearrange(
                                "p (b h f) -> p b h f", h=2, f=32)
                            t4 = spl_u[:, 0:cw].rearrange(
                                "p (b h f) -> p b h f", h=2, f=32)
                            m24 = m2t[:, 0:cw].rearrange(
                                "p (b h f) -> p b h f", h=2, f=32)
                            k.op("v", lambda e, cw=cw, m1t=m1t: e.tensor_tensor(
                                out=spl_t[:, 0:cw], in0=spl_f[:, 0:cw],
                                in1=m1t[:, 0:cw], op=ALU.mult),
                                reads=["spl_f", m1b], writes=["spl_t"])
                            k.op("v", lambda e, w4=w4, t4=t4, m24=m24:
                                 e.tensor_tensor(
                                     out=t4[:, :, 0, :], in0=w4[:, :, 1, :],
                                     in1=m24[:, :, 0, :], op=ALU.mult),
                                 reads=["spl_f", m2b], writes=["spl_u"])
                            k.op("v", lambda e, w4=w4, t4=t4, m24=m24:
                                 e.tensor_tensor(
                                     out=t4[:, :, 1, :], in0=w4[:, :, 0, :],
                                     in1=m24[:, :, 1, :], op=ALU.mult),
                                 reads=["spl_f", m2b], writes=["spl_u"])
                            k.op("v", lambda e, cw=cw: e.tensor_tensor(
                                out=spl_f[:, 0:cw], in0=spl_t[:, 0:cw],
                                in1=spl_u[:, 0:cw], op=ALU.add),
                                reads=["spl_t", "spl_u"], writes=["spl_f"])
                        k.op("v", lambda e, cw=cw: e.tensor_copy(
                            out=spl_h[:, 0:cw], in_=spl_f[:, 0:cw]),
                            reads=["spl_f"], writes=["spl_h"])
                        k.op("v", lambda e, cw=cw: e.tensor_sub(
                            spl_l[:, 0:cw], spl_f[:, 0:cw], spl_h[:, 0:cw]),
                            reads=["spl_f", "spl_h"], writes=["spl_l"])
                        k.dma("g", w_scr[0][r0:r0 + 128, c0:c0 + cw],
                              spl_h[:, 0:cw], reads=["spl_h"], writes=[("scr", id(w_scr[0]))])
                        k.dma("g", w_scr[1][r0:r0 + 128, c0:c0 + cw],
                              spl_l[:, 0:cw], reads=["spl_l"], writes=[("scr", id(w_scr[1]))])

        if upto >= 0:
            phase_pre()

        def phase_p0():
            ctx = contextlib.ExitStack()
            C = ctx.enter_context
            xg = C(nc.sbuf_tensor("xg", [128, 2, D], F32))
            xgh = C(nc.sbuf_tensor("xgh", [128, 2, D], BF16))
            xgl = C(nc.sbuf_tensor("xgl", [128, 2, D], BF16))
            xt_h = C(nc.sbuf_tensor("xt_h", [128, D // 128, TC], BF16))
            xt_l = C(nc.sbuf_tensor("xt_l", [128, D // 128, TC], BF16))
            ct_h = C(nc.sbuf_tensor("ct_h", [128, CF // 128, TC], BF16))
            ct_l = C(nc.sbuf_tensor("ct_l", [128, CF // 128, TC], BF16))
            wa_h = C(nc.sbuf_tensor("wa_h", [128, 16, 256], BF16))
            wa_l = C(nc.sbuf_tensor("wa_l", [128, 16, 256], BF16))
            wb_h = C(nc.sbuf_tensor("wb_h", [128, 4, 512], BF16))
            wb_l = C(nc.sbuf_tensor("wb_l", [128, 4, 512], BF16))
            ev_h = C(nc.sbuf_tensor("ev_h", [128, TC], BF16))
            ev_l = C(nc.sbuf_tensor("ev_l", [128, TC], BF16))
            pT = C(nc.psum_tensor("pT", [128, 512], F32))
            pT2 = C(nc.psum_tensor("pT2", [128, 512], F32))
            pA = C(nc.psum_tensor("pA", [128, 512], F32))
            pB = C(nc.psum_tensor("pB", [128, 512], F32))


            xt_reads = [("xt_h", dc) for dc in range(16)] + \
                       [("xt_l", dc) for dc in range(16)]
            ct_reads = [("ct_h", c) for c in range(4)] + \
                       [("ct_l", c) for c in range(4)]

            for tc in range(NTC):
                t0 = tc * TC
                jrot = 0
                for sub in range(TC // 128):
                    sl = sub % 2
                    k.dma("y", xg[:, sl, :],
                          x_in[t0 + sub * 128:t0 + (sub + 1) * 128, :],
                          writes=[("xg", sl)])
                    k.op("v", lambda e, sl=sl: e.tensor_copy(
                        out=xgh[:, sl, :], in_=xg[:, sl, :]),
                        reads=[("xg", sl)], writes=[("xgh", sl)])
                    k.op("v", lambda e, sl=sl: e.tensor_tensor(
                        out=xgl[:, sl, :], in0=xg[:, sl, :],
                        in1=xgh[:, sl, :], op=ALU.subtract),
                        reads=[("xg", sl), ("xgh", sl)], writes=[("xgl", sl)])
                    for dc in range(D // 128):
                        for src_t, srcb, dst_t, dstb in (
                                (xgh, ("xgh", sl), xt_h, ("xt_h", dc)),
                                (xgl, ("xgl", sl), xt_l, ("xt_l", dc))):
                            bank, bkey = ((pT, "pT0") if jrot % 2 == 0
                                          else (pT2, "pT1"))
                            jrot += 1
                            k.mm(bank[:, 0:128],
                                 src_t[:, sl, dc * 128:(dc + 1) * 128],
                                 identb[:], [srcb, "identb"], bkey, True, True)
                            k.op("v", lambda e, bank=bank, dst_t=dst_t, dc=dc,
                                 sub=sub: e.tensor_copy(
                                     out=dst_t[:, dc, sub * 128:(sub + 1) * 128],
                                     in_=bank[:, 0:128]),
                                 reads=[bkey], writes=[dstb])

                for cg in range(2):
                    k.dma("y", wa_h[:, :, :],
                          s_wkvc[0][:, cg * 256:(cg + 1) * 256].rearrange(
                              "(a p) c -> p a c", p=128), writes=["wa_h"])
                    k.dma("y", wa_l[:, :, :],
                          s_wkvc[1][:, cg * 256:(cg + 1) * 256].rearrange(
                              "(a p) c -> p a c", p=128), writes=["wa_l"])
                    for cfi in range(2):
                        cf = cg * 2 + cfi
                        for d in range(16):
                            lhh = wa_h[:, d, cfi * 128:(cfi + 1) * 128]
                            lhl = wa_l[:, d, cfi * 128:(cfi + 1) * 128]
                            k.mm(pA[:, 0:TC], lhh, xt_h[:, d, :], ["wa_h"] + xt_reads,
                                 "pA", d == 0, False)
                            k.mm(pA[:, 0:TC], lhh, xt_l[:, d, :], ["wa_h"] + xt_reads,
                                 "pA", False, False)
                            k.mm(pA[:, 0:TC], lhl, xt_h[:, d, :], ["wa_l"] + xt_reads,
                                 "pA", False, d == 15)
                        split_evac(pA[:, 0:TC], "pA", ct_h[:, cf, :], ct_l[:, cf, :],
                                   ("ct_h", cf), ("ct_l", cf))

                k.dma("y", wb_h[:, :, :],
                      s_wkdec[0].rearrange("(a p) c -> p a c", p=128),
                      writes=["wb_h"])
                k.dma("y", wb_l[:, :, :],
                      s_wkdec[1].rearrange("(a p) c -> p a c", p=128),
                      writes=["wb_l"])
                for kv in range(KVH):
                    for cf in range(4):
                        lhh = wb_h[:, cf, kv * 128:(kv + 1) * 128]
                        lhl = wb_l[:, cf, kv * 128:(kv + 1) * 128]
                        k.mm(pA[:, 0:TC], lhh, ct_h[:, cf, :], ["wb_h"] + ct_reads,
                             "pA", cf == 0, False)
                        k.mm(pA[:, 0:TC], lhh, ct_l[:, cf, :], ["wb_h"] + ct_reads,
                             "pA", False, False)
                        k.mm(pA[:, 0:TC], lhl, ct_h[:, cf, :], ["wb_l"] + ct_reads,
                             "pA", False, cf == 3)
                    split_evac(pA[:, 0:TC], "pA",
                               kn_sb[0][:, kv, t0:t0 + TC],
                               kn_sb[1][:, kv, t0:t0 + TC],
                               ("kn", kv), ("knl", kv))

                k.dma("y", wb_h[:, :, :],
                      s_wvdec[0].rearrange("(a p) c -> p a c", p=128),
                      writes=["wb_h"])
                k.dma("y", wb_l[:, :, :],
                      s_wvdec[1].rearrange("(a p) c -> p a c", p=128),
                      writes=["wb_l"])
                for sub in range(TC // 128):
                    kidx = (t0 + sub * 128) // 128
                    for cf in range(4):
                        lhh = ct_h[:, cf, sub * 128:(sub + 1) * 128]
                        lhl = ct_l[:, cf, sub * 128:(sub + 1) * 128]
                        k.mm(pB[:, 0:512], lhh, wb_h[:, cf, :], ["wb_h"] + ct_reads,
                             "pB", cf == 0, False)
                        k.mm(pB[:, 0:512], lhl, wb_h[:, cf, :], ["wb_h"] + ct_reads,
                             "pB", False, False)
                        k.mm(pB[:, 0:512], lhh, wb_l[:, cf, :], ["wb_l"] + ct_reads,
                             "pB", False, cf == 3)
                    split_evac(pB[:, 0:512], "pB",
                               v_sb[0][:, kidx, :], v_sb[1][:, kidx, :],
                               ("v", kidx), ("vl", kidx))

                k.dma("y", wb_h[:, :, 0:256],
                      s_wkr[0].rearrange("(a p) c -> p a c", p=128),
                      writes=["wb_h"])
                k.dma("y", wb_l[:, :, 0:256],
                      s_wkr[1].rearrange("(a p) c -> p a c", p=128),
                      writes=["wb_l"])
                for blk in range(KVH // 2):
                    for cf in range(4):
                        lhh = wb_h[:, cf, blk * 128:(blk + 1) * 128]
                        lhl = wb_l[:, cf, blk * 128:(blk + 1) * 128]
                        k.mm(pA[:, 0:TC], lhh, ct_h[:, cf, :], ["wb_h"] + ct_reads,
                             "pA", cf == 0, False)
                        k.mm(pA[:, 0:TC], lhh, ct_l[:, cf, :], ["wb_h"] + ct_reads,
                             "pA", False, False)
                        k.mm(pA[:, 0:TC], lhl, ct_h[:, cf, :], ["wb_l"] + ct_reads,
                             "pA", False, cf == 3)
                    split_evac(pA[:, 0:TC], "pA",
                               kr_sb[0][:, blk, t0:t0 + TC],
                               kr_sb[1][:, blk, t0:t0 + TC],
                               ("kr", blk), ("krl", blk))

                for h in range(H):
                    if h % 2 == 0:
                        k.dma("y", wa_h[:, :, :],
                              s_wq[0][:, h * HD:(h + 2) * HD].rearrange(
                                  "(a p) c -> p a c", p=128), writes=["wa_h"])
                        k.dma("y", wa_l[:, :, :],
                              s_wq[1][:, h * HD:(h + 2) * HD].rearrange(
                                  "(a p) c -> p a c", p=128), writes=["wa_l"])
                    hc = (h % 2) * 128
                    for d in range(16):
                        lhh = wa_h[:, d, hc:hc + 128]
                        lhl = wa_l[:, d, hc:hc + 128]
                        k.mm(pA[:, 0:TC], lhh, xt_h[:, d, :], ["wa_h"] + xt_reads,
                             "pA", d == 0, False)
                        k.mm(pA[:, 0:TC], lhh, xt_l[:, d, :], ["wa_h"] + xt_reads,
                             "pA", False, False)
                        k.mm(pA[:, 0:TC], lhl, xt_h[:, d, :], ["wa_l"] + xt_reads,
                             "pA", False, d == 15)
                    split_evac(pA[:, 0:TC], "pA", ev_h[:, 0:TC], ev_l[:, 0:TC],
                               "ev_h", "ev_l")
                    k.dma("g", s_qn[0][h * 128:(h + 1) * 128, t0:t0 + TC],
                          ev_h[:, 0:TC], reads=["ev_h"], writes=["scr_qn0"])
                    k.dma("g", s_qn[1][h * 128:(h + 1) * 128, t0:t0 + TC],
                          ev_l[:, 0:TC], reads=["ev_l"], writes=["scr_qn1"])

                for blk in range(H // 2):
                    if blk % 2 == 0:
                        k.dma("y", wa_h[:, :, :],
                              s_wqr[0][:, blk * 128:(blk + 2) * 128].rearrange(
                                  "(a p) c -> p a c", p=128), writes=["wa_h"])
                        k.dma("y", wa_l[:, :, :],
                              s_wqr[1][:, blk * 128:(blk + 2) * 128].rearrange(
                                  "(a p) c -> p a c", p=128), writes=["wa_l"])
                    hc = (blk % 2) * 128
                    for d in range(16):
                        lhh = wa_h[:, d, hc:hc + 128]
                        lhl = wa_l[:, d, hc:hc + 128]
                        k.mm(pA[:, 0:TC], lhh, xt_h[:, d, :], ["wa_h"] + xt_reads,
                             "pA", d == 0, False)
                        k.mm(pA[:, 0:TC], lhh, xt_l[:, d, :], ["wa_h"] + xt_reads,
                             "pA", False, False)
                        k.mm(pA[:, 0:TC], lhl, xt_h[:, d, :], ["wa_l"] + xt_reads,
                             "pA", False, d == 15)
                    split_evac(pA[:, 0:TC], "pA", ev_h[:, 0:TC], ev_l[:, 0:TC],
                               "ev_h", "ev_l")
                    k.dma("g", s_qr[0][blk * 128:(blk + 1) * 128, t0:t0 + TC],
                          ev_h[:, 0:TC], reads=["ev_h"], writes=["scr_qr0"])
                    k.dma("g", s_qr[1][blk * 128:(blk + 1) * 128, t0:t0 + TC],
                          ev_l[:, 0:TC], reads=["ev_l"], writes=["scr_qr1"])
            ctx.close()

        phase_p0()
        k.new_phase()

        def phase_p1():
            ctx = contextlib.ExitStack()
            C = ctx.enter_context
            qn_h = C(nc.sbuf_tensor("qn_h", [128, S], BF16))
            qn_l = C(nc.sbuf_tensor("qn_l", [128, S], BF16))
            qr_h = C(nc.sbuf_tensor("qr_h", [128, S], BF16))
            qr_l = C(nc.sbuf_tensor("qr_l", [128, S], BF16))
            eP = C(nc.sbuf_tensor("eP", [128, 2, 512], F32))
            n_h = C(nc.sbuf_tensor("n_h", [128, 3, 512], BF16))
            n_l = C(nc.sbuf_tensor("n_l", [128, 3, 512], BF16))
            ones1 = C(nc.sbuf_tensor("ones1", [128, 1], BF16))
            zrow = C(nc.sbuf_tensor("zrow", [128, 512], F32))
            rbs = C(nc.sbuf_tensor("rbs", [128, 512], F32))
            otmp = C(nc.sbuf_tensor("otmp", [128, 512], F32))
            ot_hb = C(nc.sbuf_tensor("ot_hb", [128, 512], BF16))
            ot_lb = C(nc.sbuf_tensor("ot_lb", [128, 512], BF16))
            pS2 = C(nc.psum_tensor("pS2", [128, 1024], F32))
            pM = C(nc.psum_tensor("pM", [128, 512], F32))
            pO = C(nc.psum_tensor("pO", [128, 512], F32))
            pR = C(nc.psum_tensor("pR", [128, 512], F32))
            dmask = C(nc.sbuf_tensor("sb_dmask", [128, 4, 512], F32))
            k.dma("y", dmask[:], dmask_in.rearrange("(a p) c -> p a c", p=128),
                  writes=["dmask"])

            k.op("v", lambda e: e.memset(zrow[:], 0.0), writes=["zrow"])
            k.op("v", lambda e: e.memset(ones1[:, 0:1], 1.0), writes=["ones1"])

            NQT = S // 512
            for h in range(H):
                kv = h // (H // KVH)
                kvb = kv // 2
                kvs = (kv % 2) * 64
                k.dma("y", qn_h[:], s_qn[0][h * 128:(h + 1) * 128, :],
                      writes=["qn_h"])
                k.dma("y", qn_l[:], s_qn[1][h * 128:(h + 1) * 128, :],
                      writes=["qn_l"])
                qrb = (h // 8) * 4 + (h % 4)
                k.dma("y", qr_h[kvs:kvs + 64, :],
                      s_qr[0][qrb * 128 + kvs:qrb * 128 + kvs + 64, :],
                      writes=["qr_h"])
                k.dma("y", qr_l[kvs:kvs + 64, :],
                      s_qr[1][qrb * 128 + kvs:qrb * 128 + kvs + 64, :],
                      writes=["qr_l"])
                for qt in range(NQT):
                    q0 = qt * 512
                    nki = 4 * qt + 4
                    rq_nh = qn_h[:, q0:q0 + 512]
                    rq_nl = qn_l[:, q0:q0 + 512]
                    rq_rh = qr_h[kvs:kvs + 64, q0:q0 + 512]
                    rq_rl = qr_l[kvs:kvs + 64, q0:q0 + 512]
                    def emit_qk(ki):
                        sbank = ki % 2
                        sap = pS2[:, sbank * 512:(sbank + 1) * 512]
                        sbk = ("pS2", sbank)
                        kn0 = kn_sb[0][:, kv, ki * 128:(ki + 1) * 128]
                        kn1 = kn_sb[1][:, kv, ki * 128:(ki + 1) * 128]
                        kr0 = kr_sb[0][kvs:kvs + 64, kvb, ki * 128:(ki + 1) * 128]
                        kr1 = kr_sb[1][kvs:kvs + 64, kvb, ki * 128:(ki + 1) * 128]
                        knb = ("kn", kv)
                        knlb = ("knl", kv)
                        krb = ("kr", kvb)
                        krlb = ("krl", kvb)
                        k.mm(sap, kn0, rq_nh, [knb, "qn_h"], sbk, True, False)
                        k.mm(sap, kn0, rq_nl, [knb, "qn_l"], sbk, False, False)
                        k.mm(sap, kn1, rq_nh, [knlb, "qn_h"], sbk, False, False)
                        k.mm(sap, kr0, rq_rh, [krb, "qr_h"], sbk, False, False)
                        k.mm(sap, kr0, rq_rl, [krb, "qr_l"], sbk, False, False)
                        k.mm(sap, kr1, rq_rh, [krlb, "qr_h"], sbk, False, True)
                        eslot = ki % 2
                        ev = eP[:, eslot, :]
                        ebk = ("eP", eslot)
                        k.op("s", lambda e, sap=sap, ev=ev: e.activation(
                            out=ev, in_=sap, func=ACT.Exp, scale=EXP_SCALE),
                            reads=[sbk], writes=[ebk])
                        j = ki - 4 * qt
                        if j >= 0:
                            k.op("v", lambda e, ev=ev, j=j: e.tensor_tensor(
                                out=ev, in0=ev, in1=dmask[:, j, :], op=ALU.mult),
                                reads=[ebk, "dmask"], writes=[ebk])
                        nslot = ki % 3
                        nhv = n_h[:, nslot, :]
                        nlv = n_l[:, nslot, :]
                        nhb = ("n_h", nslot)
                        nlb = ("n_l", nslot)
                        k.op("v", lambda e, ev=ev, nhv=nhv: e.tensor_copy(
                            out=nhv, in_=ev), reads=[ebk], writes=[nhb])
                        k.op("v", lambda e, ev=ev, nhv=nhv, nlv=nlv: e.tensor_tensor(
                            out=nlv, in0=ev, in1=nhv, op=ALU.subtract),
                            reads=[ebk, nhb], writes=[nlb])

                    def emit_pv(ki):
                        nslot = ki % 3
                        nhv = n_h[:, nslot, :]
                        nlv = n_l[:, nslot, :]
                        nhb = ("n_h", nslot)
                        nlb = ("n_l", nslot)
                        k.mm(pM[0:1, 0:512], ones1[:, 0:1], nhv,
                             ["ones1", nhb], "pM", ki == 0, False)
                        k.mm(pM[0:1, 0:512], ones1[:, 0:1], nlv,
                             ["ones1", nlb], "pM", False, ki == nki - 1)
                        vh = v_sb[0][:, ki, kv * 128:(kv + 1) * 128]
                        vl = v_sb[1][:, ki, kv * 128:(kv + 1) * 128]
                        k.mm(pO[:, 0:512], vh, nhv, [("v", ki), nhb], "pO",
                             ki == 0, False)
                        k.mm(pO[:, 0:512], vl, nhv, [("vl", ki), nhb], "pO",
                             False, False)
                        k.mm(pO[:, 0:512], vh, nlv, [("v", ki), nlb], "pO",
                             False, ki == nki - 1, inc=True)

                    emit_qk(0)
                    for ki in range(1, nki):
                        emit_qk(ki)
                        emit_pv(ki - 1)
                    emit_pv(nki - 1)
                    k.op("v", lambda e: e.reciprocal(
                        out=zrow[0:1, 0:512], in_=pM[0:1, 0:512]),
                        reads=["pM"], writes=["zrow"])
                    k.mm(pR[:, 0:512], ones[:], zrow[:, 0:512],
                         ["ones", "zrow"], "pR", True, True)
                    k.op("v", lambda e: e.tensor_copy(
                        out=rbs[:, 0:512], in_=pR[:, 0:512]),
                        reads=["pR"], writes=["rbs"])
                    k.op("v", lambda e: e.tensor_tensor(
                        out=otmp[:, 0:512], in0=pO[:, 0:512], in1=rbs[:, 0:512],
                        op=ALU.mult), reads=["pO", "rbs"], writes=["otmp"])
                    k.op("v", lambda e: e.tensor_copy(
                        out=ot_hb[:, 0:512], in_=otmp[:, 0:512]),
                        reads=["otmp"], writes=["ot_hb"])
                    k.op("v", lambda e: e.tensor_tensor(
                        out=ot_lb[:, 0:512], in0=otmp[:, 0:512],
                        in1=ot_hb[:, 0:512], op=ALU.subtract),
                        reads=["otmp", "ot_hb"], writes=["ot_lb"])
                    k.dma("g", s_ot[0][h * 128:(h + 1) * 128, q0:q0 + 512],
                          ot_hb[:, 0:512], reads=["ot_hb"], writes=["scr_ot0"])
                    k.dma("g", s_ot[1][h * 128:(h + 1) * 128, q0:q0 + 512],
                          ot_lb[:, 0:512], reads=["ot_lb"], writes=["scr_ot1"])
            ctx.close()

        phase_p1()
        ctx01.close()
        k.new_phase()

        xqT = E(nc.sbuf_tensor("xqT", [128, D // 128, S], BF16))
        s1inv = E(nc.sbuf_tensor("s1inv", [128, NSUB], F32))
        c2s = E(nc.sbuf_tensor("c2s", [128, NSUB], F32))
        is2 = E(nc.sbuf_tensor("is2", [128, NSUB], F32))
        xq_reads = [("xqT", dc) for dc in range(D // 128)]

        def phase_p2a():
            ctx = contextlib.ExitStack()
            C = ctx.enter_context
            wo2 = [C(nc.sbuf_tensor(f"wo2_{p}", [128, 16, 512], BF16))
                   for p in "hl"]
            ot2 = [C(nc.sbuf_tensor(f"ot2_{p}", [128, 16, 128], BF16))
                   for p in "hl"]
            xgh2 = C(nc.sbuf_tensor("xgh2", [128, 512], F32))
            x2g = C(nc.sbuf_tensor("x2g", [128, D], F32))
            xh0 = C(nc.sbuf_tensor("xh0", [128, 512], F32))
            lnm = C(nc.sbuf_tensor("lnm", [128, 1], F32))
            lnv = C(nc.sbuf_tensor("lnv", [128, 1], F32))
            zq = C(nc.sbuf_tensor("zq", [128, D], F32))
            cq = C(nc.sbuf_tensor("cq", [128, D], F32))
            qiq = C(nc.sbuf_tensor("qiq", [128, D], BF16))
            pZ = C(nc.psum_tensor("pZ", [128, 512], F32))
            pQT = C(nc.psum_tensor("pQT", [128, 512], F32))
            pQT2 = C(nc.psum_tensor("pQT2", [128, 512], F32))

            for half in range(4):
                c0 = half * 512
                for p in range(2):
                    k.dma("y", wo2[p][:, :, :],
                          s_wo[p][:, c0:c0 + 512].rearrange(
                              "(a p) c -> p a c", p=128),
                          writes=[("wo2", p)])
                for sub in range(NSUB):
                    t0 = sub * 128
                    for p in range(2):
                        k.dma("y", ot2[p][:, :, :],
                              s_ot[p][:, t0:t0 + 128].rearrange(
                                  "(a p) c -> p a c", p=128),
                              writes=[("ot2", p)])
                    k.dma("y", xgh2[:, :], x_in[t0:t0 + 128, c0:c0 + 512],
                          writes=["xgh2"])
                    for h in range(H):
                        combos = [(ot2[0], ("ot2", 0), wo2[0], ("wo2", 0)),
                                  (ot2[0], ("ot2", 0), wo2[1], ("wo2", 1)),
                                  (ot2[1], ("ot2", 1), wo2[0], ("wo2", 0))]
                        for pi, (lh, lhb, wt, wtb) in enumerate(combos):
                            k.mm(pZ[:, 0:512], lh[:, h, :], wt[:, h, :],
                                 [lhb, wtb], "pZ",
                                 start=(h == 0 and pi == 0),
                                 stop=(h == H - 1 and pi == 2),
                                 inc=(pi == 2))
                    k.op("v", lambda e, c0=c0: e.tensor_tensor(
                        out=x2g[:, c0:c0 + 512], in0=pZ[:, 0:512],
                        in1=xgh2[:, :], op=ALU.add),
                        reads=["pZ", "xgh2"], writes=["x2g"])
                    if half < 3:
                        k.dma("g", s_x2[t0:t0 + 128, c0:c0 + 512],
                              x2g[:, c0:c0 + 512], reads=["x2g"],
                              writes=["scr_x2"])
                        continue
                    for qq in range(3):
                        k.dma("y", xh0[:, :], s_x2[t0:t0 + 128,
                                                   qq * 512:(qq + 1) * 512],
                              reads=["scr_x2"], writes=["xh0"])
                        k.op("v", lambda e, qq=qq: e.tensor_copy(
                            out=x2g[:, qq * 512:(qq + 1) * 512], in_=xh0[:, :]),
                            reads=["xh0"], writes=["x2g"])
                    k.op("v", lambda e: e.reduce_sum(
                        out=lnm[:, 0:1], in_=x2g[:, :], axis=AX.X),
                        reads=["x2g"], writes=["lnm"])
                    k.op("v", lambda e: e.tensor_scalar(
                        out=lnm[:, 0:1], in0=lnm[:, 0:1], scalar1=1.0 / D,
                        scalar2=None, op0=ALU.mult), reads=["lnm"], writes=["lnm"])
                    k.op("v", lambda e: e.tensor_scalar(
                        out=x2g[:, :], in0=x2g[:, :], scalar1=lnm[:, 0:1],
                        scalar2=None, op0=ALU.subtract),
                        reads=["x2g", "lnm"], writes=["x2g"])
                    k.op("s", lambda e: e.square(out=zq[:, :], in_=x2g[:, :]),
                         reads=["x2g"], writes=["zq"])
                    k.op("v", lambda e: e.reduce_sum(
                        out=lnv[:, 0:1], in_=zq[:, :], axis=AX.X),
                        reads=["zq"], writes=["lnv"])
                    k.op("s", lambda e: e.activation(
                        out=lnv[:, 0:1], in_=lnv[:, 0:1], func=ACT.Sqrt,
                        bias=ceps[:, 0:1], scale=1.0 / D),
                        reads=["lnv", "ceps"], writes=["lnv"])
                    k.op("v", lambda e: e.reciprocal(
                        out=lnv[:, 0:1], in_=lnv[:, 0:1]),
                        reads=["lnv"], writes=["lnv"])
                    k.op("v", lambda e: e.tensor_scalar(
                        out=x2g[:, :], in0=x2g[:, :], scalar1=lnv[:, 0:1],
                        scalar2=None, op0=ALU.mult),
                        reads=["x2g", "lnv"], writes=["x2g"])
                    k.dma("g", s_x2[t0:t0 + 128, :], x2g[:, :],
                          reads=["x2g"], writes=["scr_x2b"])
                    k.op("v", lambda e: e.reduce_max(
                        out=lnm[:, 0:1], in_=x2g[:, :], axis=AX.X,
                        apply_absolute_value=True), reads=["x2g"], writes=["lnm"])
                    k.op("v", lambda e: e.tensor_scalar(
                        out=lnm[:, 0:1], in0=lnm[:, 0:1], scalar1=1e-5,
                        scalar2=None, op0=ALU.max), reads=["lnm"], writes=["lnm"])
                    k.op("v", lambda e, sub=sub: e.tensor_scalar(
                        out=s1inv[:, sub:sub + 1], in0=lnm[:, 0:1],
                        scalar1=1.0 / 127.0, scalar2=None, op0=ALU.mult),
                        reads=["lnm"], writes=["s1inv"])
                    k.op("v", lambda e: e.reciprocal(
                        out=lnv[:, 0:1], in_=lnm[:, 0:1]),
                        reads=["lnm"], writes=["lnv"])
                    k.op("v", lambda e: e.tensor_scalar(
                        out=lnv[:, 0:1], in0=lnv[:, 0:1], scalar1=127.0,
                        scalar2=None, op0=ALU.mult), reads=["lnv"], writes=["lnv"])
                    k.op("v", lambda e: e.tensor_scalar(
                        out=zq[:, :], in0=x2g[:, :], scalar1=lnv[:, 0:1],
                        scalar2=128.5, op0=ALU.mult, op1=ALU.add),
                        reads=["x2g", "lnv"], writes=["zq"])
                    for kb in range(7, -1, -1):
                        t = float(1 << kb)
                        k.op("v", lambda e, t=t: e.tensor_single_scalar(
                            out=cq[:, :], in_=zq[:, :], scalar=t, op=ALU.is_ge),
                            reads=["zq"], writes=["cq"])
                        k.op("v", lambda e, t=t: e.scalar_tensor_tensor(
                            out=zq[:, :], in0=cq[:, :], scalar=-t, in1=zq[:, :],
                            op0=ALU.mult, op1=ALU.add),
                            reads=["cq", "zq"], writes=["zq"])
                    k.op("v", lambda e: e.tensor_scalar(
                        out=cq[:, :], in0=x2g[:, :], scalar1=lnv[:, 0:1],
                        scalar2=0.5, op0=ALU.mult, op1=ALU.add),
                        reads=["x2g", "lnv"], writes=["cq"])
                    k.op("v", lambda e: e.tensor_tensor(
                        out=qiq[:, :], in0=cq[:, :], in1=zq[:, :],
                        op=ALU.subtract), reads=["cq", "zq"], writes=["qiq"])
                    for dc in range(D // 128):
                        bank, bkey = ((pQT, "pQT0") if dc % 2 == 0
                                      else (pQT2, "pQT1"))
                        k.mm(bank[:, 0:128], qiq[:, dc * 128:(dc + 1) * 128],
                             identb[:], ["qiq", "identb"], bkey, True, True)
                        k.op("v", lambda e, bank=bank, dc=dc, sub=sub:
                             e.tensor_copy(
                                 out=xqT[:, dc, sub * 128:(sub + 1) * 128],
                                 in_=bank[:, 0:128]),
                             reads=[bkey], writes=[("xqT", dc)])
            ctx.close()

        if upto >= 3:
            phase_p2a()
        k.new_phase()

        def phase_p2b():
            ctx = contextlib.ExitStack()
            C = ctx.enter_context
            w13f = C(nc.sbuf_tensor("w13f", [128, 16, 512], F32))
            w13h = C(nc.sbuf_tensor("w13h", [128, 16, 512], BF16))
            w13l = C(nc.sbuf_tensor("w13l", [128, 16, 512], BF16))
            msac = C(nc.sbuf_tensor("msac", [128, NSUB], F32))
            mxac = C(nc.sbuf_tensor("mxac", [128, NSUB], F32))
            rg1 = C(nc.sbuf_tensor("rg1", [128, 256], F32))
            rg2 = C(nc.sbuf_tensor("rg2", [128, 256], F32))
            rg3 = C(nc.sbuf_tensor("rg3", [128, 256], F32))
            red1 = C(nc.sbuf_tensor("red1", [128, 1], F32))
            nwbs = C(nc.sbuf_tensor("nwbs", [128, FH], F32))
            pF = C(nc.psum_tensor("pF", [128, 1024], F32))
            if True:
                w13ff = w13f.rearrange("p a b -> p (a b)")
                w13hf = w13h.rearrange("p a b -> p (a b)")
                w13lf = w13l.rearrange("p a b -> p (a b)")
                k.dma("y", nwbs[:], nwb_in, writes=["nwbs"])
                k.op("v", lambda e: e.memset(msac[:], 0.0), writes=["msac"])
                k.op("v", lambda e: e.memset(mxac[:], 0.0), writes=["mxac"])

                for fc in range(NFC):
                    k.dma("y", w13f[:, :, :],
                          w13t_in[:, fc * 512:(fc + 1) * 512].rearrange(
                              "(a p) c -> p a c", p=128), writes=["w13f"])
                    k.op("v", lambda e: e.tensor_copy(out=w13hf, in_=w13ff),
                         reads=["w13f"], writes=["w13h"])
                    k.op("v", lambda e: e.tensor_tensor(
                        out=w13lf, in0=w13ff, in1=w13hf, op=ALU.subtract),
                        reads=["w13f", "w13h"], writes=["w13l"])
                    if P2BCUT < 1:
                        continue
                    for sub in range(NSUB):
                        bank = sub % 2
                        for pi, (wt, wtb) in enumerate([(w13h, "w13h"),
                                                        (w13l, "w13l")]):
                            for d in range(16):
                                k.mm(pF[:, bank * 512:bank * 512 + 512],
                                     xqT[:, d, sub * 128:(sub + 1) * 128],
                                     wt[:, d, :], xq_reads + [wtb], ("pF", bank),
                                     start=(pi == 0 and d == 0),
                                     stop=(pi == 1 and d == 15))
                        if P2BCUT < 2:
                            continue
                        x1p = pF[:, bank * 512:bank * 512 + 256]
                        x3p = pF[:, bank * 512 + 256:bank * 512 + 512]
                        k.op("s", lambda e, x1p=x1p, sub=sub: e.activation(
                            out=rg1[:, :], in_=x1p, func=ACT.Relu,
                            scale=s1inv[:, sub:sub + 1]),
                            reads=[("pF", bank), "s1inv"], writes=["rg1"])
                        if P2BCUT < 21:
                            continue
                        k.op("s", lambda e: e.square(out=rg2[:, :], in_=rg1[:, :]),
                             reads=["rg1"], writes=["rg2"])
                        if P2BCUT < 22:
                            continue
                        k.op("s", lambda e, x3p=x3p: e.copy(
                            out=rg3[:, :], in_=x3p),
                            reads=[("pF", bank)], writes=["rg3"])
                        if P2BCUT < 23:
                            continue
                        k.op("v", lambda e, sub=sub: e.scalar_tensor_tensor(
                            out=rg1[:, :], in0=rg2[:, :],
                            scalar=s1inv[:, sub:sub + 1], in1=rg3[:, :],
                            op0=ALU.mult, op1=ALU.mult),
                            reads=["rg2", "rg3", "s1inv"], writes=["rg1"])
                        if P2BCUT < 24:
                            continue

                        k.op("s", lambda e: e.square(out=rg2[:, :], in_=rg1[:, :]),
                             reads=["rg1"], writes=["rg2"])
                        k.op("v", lambda e: e.reduce_sum(
                            out=red1[:, 0:1], in_=rg2[:, :], axis=AX.X),
                            reads=["rg2"], writes=["red1"])
                        k.op("v", lambda e, sub=sub: e.tensor_tensor(
                            out=msac[:, sub:sub + 1], in0=msac[:, sub:sub + 1],
                            in1=red1[:, 0:1], op=ALU.add),
                            reads=["msac", "red1"], writes=["msac"])
                        k.op("v", lambda e, fc=fc: e.tensor_tensor(
                            out=rg3[:, :], in0=rg1[:, :],
                            in1=nwbs[:, fc * 256:(fc + 1) * 256], op=ALU.mult),
                            reads=["rg1", "nwbs"], writes=["rg3"])
                        k.op("v", lambda e: e.reduce_max(
                            out=red1[:, 0:1], in_=rg3[:, :], axis=AX.X,
                            apply_absolute_value=True),
                            reads=["rg3"], writes=["red1"])
                        k.op("v", lambda e, sub=sub: e.tensor_tensor(
                            out=mxac[:, sub:sub + 1], in0=mxac[:, sub:sub + 1],
                            in1=red1[:, 0:1], op=ALU.max),
                            reads=["mxac", "red1"], writes=["mxac"])
                        if P2BCUT >= 4:
                            k.dma("g", s_gnw[sub * 128:(sub + 1) * 128,
                                            fc * 256:(fc + 1) * 256],
                                  rg3[:, :], reads=["rg3"], writes=["scr_gnw"])

                for sub in range(NSUB):
                    k.op("s", lambda e, sub=sub: e.activation(
                        out=c2s[:, sub:sub + 1], in_=msac[:, sub:sub + 1],
                        func=ACT.Sqrt, bias=ceps[:, 0:1], scale=1.0 / FH),
                        reads=["msac", "ceps"], writes=[("c2s", sub)])
                    k.op("v", lambda e, sub=sub: e.reciprocal(
                        out=c2s[:, sub:sub + 1], in_=c2s[:, sub:sub + 1]),
                        reads=[("c2s", sub)], writes=[("c2s", sub)])
                    k.op("v", lambda e, sub=sub: e.tensor_tensor(
                        out=is2[:, sub:sub + 1], in0=mxac[:, sub:sub + 1],
                        in1=c2s[:, sub:sub + 1], op=ALU.mult),
                        reads=["mxac", ("c2s", sub)], writes=[("is2", sub)])
                    k.op("v", lambda e, sub=sub: e.tensor_scalar(
                        out=is2[:, sub:sub + 1], in0=is2[:, sub:sub + 1],
                        scalar1=1e-5, scalar2=None, op0=ALU.max),
                        reads=[("is2", sub)], writes=[("is2", sub)])
                    k.op("v", lambda e, sub=sub: e.reciprocal(
                        out=red1[:, 0:1], in_=is2[:, sub:sub + 1]),
                        reads=[("is2", sub)], writes=["red1"])
                    k.op("v", lambda e: e.tensor_scalar(
                        out=red1[:, 0:1], in0=red1[:, 0:1], scalar1=127.0,
                        scalar2=None, op0=ALU.mult), reads=["red1"], writes=["red1"])
                    k.op("v", lambda e, sub=sub: e.tensor_tensor(
                        out=c2s[:, sub:sub + 1], in0=red1[:, 0:1],
                        in1=c2s[:, sub:sub + 1], op=ALU.mult),
                        reads=["red1", ("c2s", sub)], writes=[("c2s", sub)])
                    k.op("v", lambda e, sub=sub: e.tensor_scalar(
                        out=is2[:, sub:sub + 1], in0=is2[:, sub:sub + 1],
                        scalar1=1.0 / 127.0, scalar2=None, op0=ALU.mult),
                        reads=[("is2", sub)], writes=[("is2", sub)])
            ctx.close()

        if upto >= 4:
            phase_p2b()
        k.new_phase()

        def phase_p2c():
            ctx = contextlib.ExitStack()
            C = ctx.enter_context
            gbuf = C(nc.sbuf_tensor("gbuf", [128, 512], F32))
            zq2 = C(nc.sbuf_tensor("zq2", [128, 512], F32))
            cq2 = C(nc.sbuf_tensor("cq2", [128, 512], F32))
            qi2 = C(nc.sbuf_tensor("qi2", [128, 512], BF16))
            iqT = C(nc.sbuf_tensor("iqT", [128, NFH, 512], BF16))
            w2f = [C(nc.sbuf_tensor(f"w2f{i}", [128, 512], F32)) for i in range(2)]
            w2h = [C(nc.sbuf_tensor(f"w2h{i}", [128, 512], BF16)) for i in range(2)]
            w2l = [C(nc.sbuf_tensor(f"w2l{i}", [128, 512], BF16)) for i in range(2)]
            x2b = C(nc.sbuf_tensor("x2b", [128, 512], F32))
            y2 = C(nc.sbuf_tensor("y2", [128, 16, 512], F32))
            lnm2 = C(nc.sbuf_tensor("lnm2", [128, 1], F32))
            lnv2 = C(nc.sbuf_tensor("lnv2", [128, 1], F32))
            sqt2 = C(nc.sbuf_tensor("sqt2", [128, D], F32))
            pG = C(nc.psum_tensor("pG", [128, 2048], F32))
            pQ2 = C(nc.psum_tensor("pQ2", [128, 512], F32))
            pQ22 = C(nc.psum_tensor("pQ22", [128, 512], F32))
            if True:
                for sg in range(NSUB // 4):
                    subs = [sg * 4 + i for i in range(4)]
                    for si, sub in enumerate(subs):
                        for f0 in range(0, FH, 512):
                            k.dma("y", gbuf[:, :],
                                  s_gnw[sub * 128:(sub + 1) * 128, f0:f0 + 512],
                                  writes=["gbuf"])
                            k.op("v", lambda e, sub=sub: e.tensor_scalar(
                                out=zq2[:, :], in0=gbuf[:, :],
                                scalar1=c2s[:, sub:sub + 1], scalar2=128.5,
                                op0=ALU.mult, op1=ALU.add),
                                reads=["gbuf", ("c2s", sub)], writes=["zq2"])
                            for kb in range(7, -1, -1):
                                t = float(1 << kb)
                                k.op("v", lambda e, t=t: e.tensor_single_scalar(
                                    out=cq2[:, :], in_=zq2[:, :], scalar=t,
                                    op=ALU.is_ge), reads=["zq2"], writes=["cq2"])
                                k.op("v", lambda e, t=t: e.scalar_tensor_tensor(
                                    out=zq2[:, :], in0=cq2[:, :], scalar=-t,
                                    in1=zq2[:, :], op0=ALU.mult, op1=ALU.add),
                                    reads=["cq2", "zq2"], writes=["zq2"])
                            k.op("v", lambda e, sub=sub: e.tensor_scalar(
                                out=cq2[:, :], in0=gbuf[:, :],
                                scalar1=c2s[:, sub:sub + 1], scalar2=0.5,
                                op0=ALU.mult, op1=ALU.add),
                                reads=["gbuf", ("c2s", sub)], writes=["cq2"])
                            k.op("v", lambda e: e.tensor_tensor(
                                out=qi2[:, :], in0=cq2[:, :], in1=zq2[:, :],
                                op=ALU.subtract), reads=["cq2", "zq2"],
                                writes=["qi2"])
                            for j in range(4):
                                fch = f0 // 128 + j
                                bank, bkey = ((pQ2, "pQ20") if j % 2 == 0
                                              else (pQ22, "pQ21"))
                                k.mm(bank[:, 0:128],
                                     qi2[:, j * 128:(j + 1) * 128],
                                     identb[:], ["qi2", "identb"], bkey,
                                     True, True)
                                k.op("v", lambda e, bank=bank, j=j, fch=fch,
                                     si=si: e.tensor_copy(
                                         out=iqT[:, fch, si * 128:(si + 1) * 128],
                                         in_=bank[:, 0:128]),
                                     reads=[bkey], writes=[("iqT", fch)])
                    iq_reads = [("iqT", f) for f in range(NFH)]
                    for dmc in range(4):
                        for fch in range(NFH):
                            bi = fch % 2
                            q = "y" if fch % 2 == 0 else "g"
                            k.dma(q, w2f[bi][:, :],
                                  w2t_in[fch * 128:(fch + 1) * 128,
                                         dmc * 512:(dmc + 1) * 512],
                                  writes=[("w2f", bi)])
                            k.op("v", lambda e, bi=bi: e.tensor_copy(
                                out=w2h[bi][:, :], in_=w2f[bi][:, :]),
                                reads=[("w2f", bi)], writes=[("w2h", bi)])
                            k.op("v", lambda e, bi=bi: e.tensor_tensor(
                                out=w2l[bi][:, :], in0=w2f[bi][:, :],
                                in1=w2h[bi][:, :], op=ALU.subtract),
                                reads=[("w2f", bi), ("w2h", bi)],
                                writes=[("w2l", bi)])
                            for si in range(4):
                                k.mm(pG[:, si * 512:(si + 1) * 512],
                                     iqT[:, fch, si * 128:(si + 1) * 128],
                                     w2h[bi][:, :], iq_reads + [("w2h", bi)],
                                     ("pG", si), start=(fch == 0), stop=False)
                                k.mm(pG[:, si * 512:(si + 1) * 512],
                                     iqT[:, fch, si * 128:(si + 1) * 128],
                                     w2l[bi][:, :], iq_reads + [("w2l", bi)],
                                     ("pG", si), False, stop=(fch == NFH - 1),
                                     inc=(si == 3))
                        for si, sub in enumerate(subs):
                            k.dma("y", x2b[:, :],
                                  s_x2[sub * 128:(sub + 1) * 128,
                                       dmc * 512:(dmc + 1) * 512],
                                  writes=["x2b"])
                            k.op("v", lambda e, si=si, sub=sub, dmc=dmc:
                                 e.scalar_tensor_tensor(
                                     out=y2[:, si * 4 + dmc, :],
                                     in0=pG[:, si * 512:(si + 1) * 512],
                                     scalar=is2[:, sub:sub + 1], in1=x2b[:, :],
                                     op0=ALU.mult, op1=ALU.add),
                                 reads=[("pG", si), ("is2", sub), "x2b"],
                                 writes=[("y2", si, dmc)])
                    for si, sub in enumerate(subs):
                        yv = y2[:, si * 4:(si + 1) * 4, :].rearrange(
                            "p a b -> p (a b)")
                        yr = [("y2", si, d) for d in range(4)]
                        k.op("v", lambda e, yv=yv: e.reduce_sum(
                            out=lnm2[:, 0:1], in_=yv, axis=AX.X),
                            reads=yr, writes=["lnm2"])
                        k.op("v", lambda e: e.tensor_scalar(
                            out=lnm2[:, 0:1], in0=lnm2[:, 0:1], scalar1=1.0 / D,
                            scalar2=None, op0=ALU.mult),
                            reads=["lnm2"], writes=["lnm2"])
                        k.op("v", lambda e, yv=yv: e.tensor_scalar(
                            out=yv, in0=yv, scalar1=lnm2[:, 0:1], scalar2=None,
                            op0=ALU.subtract), reads=yr + ["lnm2"], writes=yr)
                        k.op("s", lambda e, yv=yv: e.square(
                            out=sqt2[:, :], in_=yv), reads=yr, writes=["sqt2"])
                        k.op("v", lambda e: e.reduce_sum(
                            out=lnv2[:, 0:1], in_=sqt2[:, :], axis=AX.X),
                            reads=["sqt2"], writes=["lnv2"])
                        k.op("s", lambda e: e.activation(
                            out=lnv2[:, 0:1], in_=lnv2[:, 0:1], func=ACT.Sqrt,
                            bias=ceps[:, 0:1], scale=1.0 / D),
                            reads=["lnv2", "ceps"], writes=["lnv2"])
                        k.op("v", lambda e: e.reciprocal(
                            out=lnv2[:, 0:1], in_=lnv2[:, 0:1]),
                            reads=["lnv2"], writes=["lnv2"])
                        k.op("v", lambda e, yv=yv: e.tensor_scalar(
                            out=yv, in0=yv, scalar1=lnv2[:, 0:1], scalar2=None,
                            op0=ALU.mult), reads=yr + ["lnv2"], writes=yr)
                        k.dma("g", out_t[sub * 128:(sub + 1) * 128, :], yv,
                              reads=yr, writes=["scr_out"])
            ctx.close()

        if upto >= 5:
            phase_p2c()
        k.wait_all_on("y")

    return nc


def _rope_factor_tables(pairs):
    theta = 10000.0
    freqs = (1.0 / theta ** (np.arange(0, RD, 2, dtype=np.float32)
                             / np.float32(RD))).astype(np.float32)
    t = np.arange(4096, dtype=np.float32)
    f = np.outer(t, freqs).astype(np.float32)
    cos = np.cos(f).astype(np.float32)
    sin = np.sin(f).astype(np.float32)
    m1 = []
    m2 = []
    for ha, hb in pairs:
        for h in (ha, hb):
            m1.append(cos[h]); m1.append(cos[h])
            m2.append(-sin[h]); m2.append(sin[h])
    m1 = np.concatenate(m1).astype(np.float32)[None, :]
    m2 = np.concatenate(m2).astype(np.float32)[None, :]
    return (np.ascontiguousarray(np.broadcast_to(m1, (128, m1.shape[1]))),
            np.ascontiguousarray(np.broadcast_to(m2, (128, m2.shape[1]))))


def _perm_rope_cols(n_heads, pairs=None):
    if pairs is None:
        pairs = [(2 * b, 2 * b + 1) for b in range(n_heads // 2)]
    perm = []
    for ha, hb in pairs:
        for h in (ha, hb):
            perm.extend((h * RD + 2 * np.arange(32)).tolist())
            perm.extend((h * RD + 2 * np.arange(32) + 1).tolist())
    return np.asarray(perm)


_Q_PAIRS = [(0, 4), (1, 5), (2, 6), (3, 7), (8, 12), (9, 13), (10, 14), (11, 15)]


def _diag_masks():
    k = np.arange(128)[:, None]
    q = np.arange(512)[None, :]
    return np.ascontiguousarray(np.concatenate(
        [(q >= k + 128 * j).astype(np.float32) for j in range(4)], axis=0))


def _prep_shared(Wq, Wkvc, Wkdec, Wvdec, Wqrope, Wkrope, Wo, w13, w2, norm_w, S):
    f32 = np.float32
    wq = np.ascontiguousarray(np.asarray(Wq, dtype=f32))
    wqr = np.ascontiguousarray(np.asarray(Wqrope, dtype=f32)[:, _perm_rope_cols(H, _Q_PAIRS)])
    wkvc = np.ascontiguousarray(np.asarray(Wkvc, dtype=f32))
    wkdec = np.ascontiguousarray(np.asarray(Wkdec, dtype=f32))
    wvdec = np.ascontiguousarray(np.asarray(Wvdec, dtype=f32))
    wkr = np.ascontiguousarray(np.asarray(Wkrope, dtype=f32)[:, _perm_rope_cols(KVH)])
    wo_full = np.asarray(Wo, dtype=f32)
    rows = np.concatenate([np.arange(h * (HD + RD), h * (HD + RD) + HD)
                           for h in range(H)])
    wo = np.ascontiguousarray(wo_full[rows, :])
    w13t = np.asarray(w13, dtype=f32).T
    blocks = []
    for j in range(FH // 256):
        blocks.append(w13t[:, j * 256:(j + 1) * 256])
        blocks.append(w13t[:, FH + j * 256:FH + (j + 1) * 256])
    w13t_i = np.ascontiguousarray(np.concatenate(blocks, axis=1))
    w2t = np.ascontiguousarray(np.asarray(w2, dtype=f32).T)
    nwb = np.ascontiguousarray(np.broadcast_to(
        np.asarray(norm_w, dtype=f32)[None, :], (128, FH)))
    m1q, m2q = _rope_factor_tables(_Q_PAIRS)
    m1k, m2k = _rope_factor_tables([(0, 1), (2, 3)])
    return dict(wq=wq, wqr=wqr, wkvc=wkvc, wkdec=wkdec, wvdec=wvdec, wkr=wkr,
                wo=wo, w13t=w13t_i, w2t=w2t, nwb=nwb,
                ident=np.eye(128, dtype=f32),
                identb=np.eye(128, dtype=ml_dtypes.bfloat16),
                dmask=_diag_masks(),
                ones=np.ones((128, 128), dtype=f32),
                m1q=m1q, m2q=m2q, m1k=m1k, m2k=m2k)


def kernel(L0_attn_hidden_states, L0_attn_Wq, L0_attn_Wkvc, L0_attn_Wkdec,
           L0_attn_Wvdec, L0_attn_Wqrope, L0_attn_Wkrope, L0_attn_Wo,
           L0_ffn_w13, L0_ffn_w2, L0_ffn_norm_w):
    x = np.ascontiguousarray(np.asarray(L0_attn_hidden_states, dtype=np.float32))
    B, S, _ = x.shape
    shared = _prep_shared(L0_attn_Wq, L0_attn_Wkvc, L0_attn_Wkdec, L0_attn_Wvdec,
                          L0_attn_Wqrope, L0_attn_Wkrope, L0_attn_Wo,
                          L0_ffn_w13, L0_ffn_w2, L0_ffn_norm_w, S)
    nc = build_core_program(S)
    in_maps = []
    for b in range(B):
        m = dict(shared)
        m["x"] = np.ascontiguousarray(x[b])
        in_maps.append(m)
    res = run_bass_kernel_spmd(nc, in_maps, core_ids=list(range(B)))
    return np.stack([np.asarray(res.results[b]["out"]) for b in range(B)], axis=0)


# version 08
# speedup vs baseline: 1.4024x; 1.0544x over previous
import contextlib
import math

import numpy as np
import ml_dtypes

import concourse.bass as bass
import concourse.mybir as mybir
from concourse.bass_utils import run_bass_kernel_spmd

F32 = mybir.dt.float32
BF16 = mybir.dt.bfloat16
ALU = mybir.AluOpType
ACT = mybir.ActivationFunctionType
AX = mybir.AxisListType

D = 2048
H = 16
KVH = 4
HD = 128
RD = 64
CF = 512
FH = 5632
LN_EPS = 1e-5
P1H = 99
P1QC = 99
P1CUT = 99
P2BCUT = 99
EXP_SCALE = float(np.float32(1.0) / np.sqrt(np.float32(HD + RD)))


class KB:
    ENG = ("t", "v", "s", "g", "y")

    def __init__(self, nc, st):
        self.nc = nc
        self.st = st
        self.eng = {"t": nc.tensor, "v": nc.vector, "s": nc.scalar,
                    "g": nc.gpsimd, "y": nc.sync}
        self.sems = {}
        self.cnt = {}
        self.waited = {e: {} for e in self.ENG}
        self.state = {}
        self.pending_reads = []
        self.dma_slots = {}
        self.phase = 0
        self.barrier_snapshot = None
        self.barrier_done = set()

    def _sem(self, key):
        if key not in self.sems:
            self.sems[key] = self.st.enter_context(
                self.nc.semaphore(f"sm{key[0]}{key[1]}{key[2]}"))
            self.cnt[key] = 0
        return self.sems[key]

    def new_phase(self):
        self.barrier_snapshot = {k: v for k, v in self.cnt.items() if v > 0}
        self.phase += 1
        self.barrier_done = set()
        self.state = {}

    def _apply_barrier(self, e):
        if self.barrier_snapshot is None or e in self.barrier_done:
            return
        self.barrier_done.add(e)
        for kk, v in self.barrier_snapshot.items():
            self._wait(e, kk, v)

    def _wait(self, e, semkey, val):
        if val <= 0:
            return
        w = self.waited[e]
        if w.get(semkey, 0) >= val:
            return
        w[semkey] = val
        self.eng[e].wait_ge(self._sem(semkey), val)

    def _dep_waits(self, e, reads, writes):
        self._apply_barrier(e)
        need = {}
        for b in reads:
            s = self.state.get(b)
            if s and s["w"]:
                kk, v = s["w"]
                need[kk] = max(need.get(kk, 0), v)
        for b in writes:
            s = self.state.get(b)
            if s:
                if s["w"]:
                    kk, v = s["w"]
                    need[kk] = max(need.get(kk, 0), v)
                for kk, v in s["r"].items():
                    need[kk] = max(need.get(kk, 0), v)
        for kk, v in need.items():
            self._wait(e, kk, v)

    def _note(self, semkey, val, reads, writes):
        for b in writes:
            self.state[b] = {"w": (semkey, val), "r": {}}
        for b in reads:
            s = self.state.setdefault(b, {"w": None, "r": {}})
            s["r"][semkey] = max(s["r"].get(semkey, 0), val)

    def op(self, e, fn, reads=(), writes=()):
        reads, writes = list(reads), list(writes)
        self._dep_waits(e, reads, writes)
        instr = fn(self.eng[e])
        kk = ("e", e, self.phase)
        self._sem(kk)
        self.cnt[kk] += 1
        instr.then_inc(self.sems[kk], 1)
        self._note(kk, self.cnt[kk], reads, writes)
        return instr

    def mm(self, psum_ap, lhsT_ap, rhs_ap, reads, psum_buf, start, stop,
           inc=False):
        reads = list(reads)
        self._dep_waits("t", reads, [psum_buf] if start else [])
        instr = self.eng["t"].matmul(psum_ap, lhsT_ap, rhs_ap,
                                     start=start, stop=stop)
        kk = ("e", "t", self.phase)
        self._sem(kk)
        if stop or inc:
            self.cnt[kk] += 1
            instr.then_inc(self.sems[kk], 1)
            self._note(kk, self.cnt[kk], reads,
                       [psum_buf] if stop else [])
        else:
            self._note(kk, self.cnt[kk] + 1, reads, [])
        return instr

    def transpose(self, psum_ap, in_ap, ident_ap, reads, psum_buf):
        reads = list(reads)
        self._dep_waits("t", reads, [psum_buf])
        instr = self.eng["t"].transpose(psum_ap, in_ap, ident_ap)
        kk = ("e", "t", self.phase)
        self._sem(kk)
        self.cnt[kk] += 1
        instr.then_inc(self.sems[kk], 1)
        self._note(kk, self.cnt[kk], reads, [psum_buf])
        return instr

    def dma(self, q, out_ap, in_ap, reads=(), writes=()):
        reads, writes = list(reads), list(writes)
        assert writes, "every DMA must declare a destination buffer key"
        self._dep_waits(q, reads, writes)
        instr = self.eng[q].dma_start(out=out_ap, in_=in_ap)
        slot = self.dma_slots.setdefault((self.phase, writes[0]),
                                         len(self.dma_slots))
        kk = ("d", slot, self.phase)
        self._sem(kk)
        self.cnt[kk] += 16
        instr.then_inc(self.sems[kk], 16)
        self._note(kk, self.cnt[kk], reads, writes)
        return instr

    def wait_all_on(self, e):
        for kk, v in list(self.cnt.items()):
            if v > 0:
                self._wait(e, kk, v)


def build_core_program(S, upto=99):
    TC = 512
    NTC = S // TC
    NQC = S // 128
    NSUB = S // 128
    NFC = 2 * FH // 512
    NFH = FH // 128
    nc = bass.Bass("TRN2", target_bir_lowering=False, debug=False)

    def din(name, shape, dt=F32):
        return nc.dram_tensor(name, list(shape), dt, kind="ExternalInput").ap()

    x_in = din("x", (S, D))
    wq_in = din("wq", (D, H * HD))
    wqr_in = din("wqr", (D, H * RD))
    wkvc_in = din("wkvc", (D, CF))
    wkdec_in = din("wkdec", (CF, KVH * HD))
    wvdec_in = din("wvdec", (CF, KVH * HD))
    wkr_in = din("wkr", (CF, KVH * RD))
    wo_in = din("wo", (H * HD, D))
    w13t_in = din("w13t", (D, 2 * FH))
    w2t_in = din("w2t", (FH, D))
    nwb_in = din("nwb", (128, FH))
    ident_in = din("ident", (128, 128))
    identb_in = din("identb", (128, 128), BF16)
    dmask_in = din("dmask", (512, 512))
    ones_in = din("ones", (128, 128))
    m1q_in = din("m1q", (128, H * RD))
    m2q_in = din("m2q", (128, H * RD))
    m1k_in = din("m1k", (128, KVH * RD))
    m2k_in = din("m2k", (128, KVH * RD))
    out_t = nc.dram_tensor("out", [S, D], F32, kind="ExternalOutput").ap()

    def scr(name, shape, dt=BF16):
        return nc.dram_tensor(name, list(shape), dt, kind="Internal").ap()

    s_wq = [scr("s_wq_h", (D, H * HD)), scr("s_wq_l", (D, H * HD))]
    s_wqr = [scr("s_wqr_h", (D, H * RD)), scr("s_wqr_l", (D, H * RD))]
    s_wkvc = [scr("s_wkvc_h", (D, CF)), scr("s_wkvc_l", (D, CF))]
    s_wkdec = [scr("s_wkdec_h", (CF, KVH * HD)), scr("s_wkdec_l", (CF, KVH * HD))]
    s_wvdec = [scr("s_wvdec_h", (CF, KVH * HD)), scr("s_wvdec_l", (CF, KVH * HD))]
    s_wkr = [scr("s_wkr_h", (CF, KVH * RD)), scr("s_wkr_l", (CF, KVH * RD))]
    s_wo = [scr("s_wo_h", (H * HD, D)), scr("s_wo_l", (H * HD, D))]
    s_qn = [scr("s_qn_h", (H * HD, S)), scr("s_qn_l", (H * HD, S))]
    s_qr = [scr("s_qr_h", (H * RD, S)), scr("s_qr_l", (H * RD, S))]
    s_ot = [scr("s_ot_h", (H * HD, S)), scr("s_ot_l", (H * HD, S))]
    s_x2 = scr("s_x2", (S, D), F32)
    s_gnw = scr("s_gnw", (S, FH), F32)

    st = contextlib.ExitStack()
    with st:
        E = st.enter_context
        k = KB(nc, st)

        identb = E(nc.sbuf_tensor("sb_identb", [128, 128], BF16))

        ones = E(nc.sbuf_tensor("sb_ones", [128, 128], F32))
        c127 = E(nc.sbuf_tensor("c127", [128, 1], F32))
        ceps = E(nc.sbuf_tensor("ceps", [128, 1], F32))
        ctx01 = contextlib.ExitStack()
        kn_sb = [ctx01.enter_context(nc.sbuf_tensor(f"kn_{p}", [128, KVH, S], BF16)) for p in "hl"]
        kr_sb = [ctx01.enter_context(nc.sbuf_tensor(f"kr_{p}", [128, KVH // 2, S], BF16)) for p in "hl"]
        v_sb = [ctx01.enter_context(nc.sbuf_tensor(f"v_{p}", [128, S // 128, KVH * HD], BF16)) for p in "hl"]

        k.dma("y", identb[:], identb_in, writes=["identb"])

        k.dma("y", ones[:], ones_in, writes=["ones"])
        k.op("v", lambda e: e.memset(c127[:, 0:1], 127.0), writes=["c127"])
        k.op("v", lambda e: e.memset(ceps[:, 0:1], LN_EPS), writes=["ceps"])

        def split_evac(psum_ap, psum_buf, hi_ap, lo_ap, hi_buf, lo_buf):
            k.op("v", lambda e: e.tensor_copy(out=hi_ap, in_=psum_ap),
                 reads=[psum_buf], writes=[hi_buf])
            k.op("v", lambda e: e.tensor_tensor(
                out=lo_ap, in0=psum_ap, in1=hi_ap, op=ALU.subtract),
                reads=[psum_buf, hi_buf], writes=[lo_buf])

        def phase_pre():
            ctx = ctx01
            C = ctx.enter_context
            spl_f = C(nc.sbuf_tensor("spl_f", [128, 1024], F32))
            spl_h = C(nc.sbuf_tensor("spl_h", [128, 1024], BF16))
            spl_l = C(nc.sbuf_tensor("spl_l", [128, 1024], BF16))
            spl_t = C(nc.sbuf_tensor("spl_t", [128, 1024], F32))
            spl_u = C(nc.sbuf_tensor("spl_u", [128, 1024], F32))
            m1q_sb = C(nc.sbuf_tensor("m1q_sb", [128, H * RD], F32))
            m2q_sb = C(nc.sbuf_tensor("m2q_sb", [128, H * RD], F32))
            m1k_sb = C(nc.sbuf_tensor("m1k_sb", [128, KVH * RD], F32))
            m2k_sb = C(nc.sbuf_tensor("m2k_sb", [128, KVH * RD], F32))
            k.dma("y", m1q_sb[:], m1q_in, writes=["m1q"])
            k.dma("y", m2q_sb[:], m2q_in, writes=["m2q"])
            k.dma("y", m1k_sb[:], m1k_in, writes=["m1k"])
            k.dma("y", m2k_sb[:], m2k_in, writes=["m2k"])
            specs = [(wq_in, s_wq, D, H * HD, None),
                     (wqr_in, s_wqr, D, H * RD, (m1q_sb, m2q_sb, "m1q", "m2q")),
                     (wkvc_in, s_wkvc, D, CF, None),
                     (wkdec_in, s_wkdec, CF, KVH * HD, None),
                     (wvdec_in, s_wvdec, CF, KVH * HD, None),
                     (wkr_in, s_wkr, CF, KVH * RD, (m1k_sb, m2k_sb, "m1k", "m2k")),
                     (wo_in, s_wo, H * HD, D, None)]
            for w_in, w_scr, R, C, ropem in specs:
                for r0 in range(0, R, 128):
                    for c0 in range(0, C, 1024):
                        cw = min(1024, C - c0)
                        k.dma("y", spl_f[:, 0:cw], w_in[r0:r0 + 128, c0:c0 + cw],
                              writes=["spl_f"])
                        if ropem is not None:
                            m1t, m2t, m1b, m2b = ropem
                            nb = cw // 64
                            w4 = spl_f[:, 0:cw].rearrange(
                                "p (b h f) -> p b h f", h=2, f=32)
                            t4 = spl_u[:, 0:cw].rearrange(
                                "p (b h f) -> p b h f", h=2, f=32)
                            m24 = m2t[:, 0:cw].rearrange(
                                "p (b h f) -> p b h f", h=2, f=32)
                            k.op("v", lambda e, cw=cw, m1t=m1t: e.tensor_tensor(
                                out=spl_t[:, 0:cw], in0=spl_f[:, 0:cw],
                                in1=m1t[:, 0:cw], op=ALU.mult),
                                reads=["spl_f", m1b], writes=["spl_t"])
                            k.op("v", lambda e, w4=w4, t4=t4, m24=m24:
                                 e.tensor_tensor(
                                     out=t4[:, :, 0, :], in0=w4[:, :, 1, :],
                                     in1=m24[:, :, 0, :], op=ALU.mult),
                                 reads=["spl_f", m2b], writes=["spl_u"])
                            k.op("v", lambda e, w4=w4, t4=t4, m24=m24:
                                 e.tensor_tensor(
                                     out=t4[:, :, 1, :], in0=w4[:, :, 0, :],
                                     in1=m24[:, :, 1, :], op=ALU.mult),
                                 reads=["spl_f", m2b], writes=["spl_u"])
                            k.op("v", lambda e, cw=cw: e.tensor_tensor(
                                out=spl_f[:, 0:cw], in0=spl_t[:, 0:cw],
                                in1=spl_u[:, 0:cw], op=ALU.add),
                                reads=["spl_t", "spl_u"], writes=["spl_f"])
                        k.op("v", lambda e, cw=cw: e.tensor_copy(
                            out=spl_h[:, 0:cw], in_=spl_f[:, 0:cw]),
                            reads=["spl_f"], writes=["spl_h"])
                        k.op("v", lambda e, cw=cw: e.tensor_sub(
                            spl_l[:, 0:cw], spl_f[:, 0:cw], spl_h[:, 0:cw]),
                            reads=["spl_f", "spl_h"], writes=["spl_l"])
                        k.dma("g", w_scr[0][r0:r0 + 128, c0:c0 + cw],
                              spl_h[:, 0:cw], reads=["spl_h"], writes=[("scr", id(w_scr[0]))])
                        k.dma("g", w_scr[1][r0:r0 + 128, c0:c0 + cw],
                              spl_l[:, 0:cw], reads=["spl_l"], writes=[("scr", id(w_scr[1]))])

        if upto >= 0:
            phase_pre()

        def phase_p0():
            ctx = contextlib.ExitStack()
            C = ctx.enter_context
            xg = C(nc.sbuf_tensor("xg", [128, 2, D], F32))
            xgh = C(nc.sbuf_tensor("xgh", [128, 2, D], BF16))
            xgl = C(nc.sbuf_tensor("xgl", [128, 2, D], BF16))
            xt_h = C(nc.sbuf_tensor("xt_h", [128, D // 128, TC], BF16))
            xt_l = C(nc.sbuf_tensor("xt_l", [128, D // 128, TC], BF16))
            ct_h = C(nc.sbuf_tensor("ct_h", [128, CF // 128, TC], BF16))
            ct_l = C(nc.sbuf_tensor("ct_l", [128, CF // 128, TC], BF16))
            wa_h = C(nc.sbuf_tensor("wa_h", [128, 16, 256], BF16))
            wa_l = C(nc.sbuf_tensor("wa_l", [128, 16, 256], BF16))
            wb_h = C(nc.sbuf_tensor("wb_h", [128, 4, 512], BF16))
            wb_l = C(nc.sbuf_tensor("wb_l", [128, 4, 512], BF16))
            ev_h = C(nc.sbuf_tensor("ev_h", [128, TC], BF16))
            ev_l = C(nc.sbuf_tensor("ev_l", [128, TC], BF16))
            pT = C(nc.psum_tensor("pT", [128, 512], F32))
            pT2 = C(nc.psum_tensor("pT2", [128, 512], F32))
            pA = C(nc.psum_tensor("pA", [128, 512], F32))
            pB = C(nc.psum_tensor("pB", [128, 512], F32))


            xt_reads = [("xt_h", dc) for dc in range(16)] + \
                       [("xt_l", dc) for dc in range(16)]
            ct_reads = [("ct_h", c) for c in range(4)] + \
                       [("ct_l", c) for c in range(4)]

            for tc in range(NTC):
                t0 = tc * TC
                jrot = 0
                for sub in range(TC // 128):
                    sl = sub % 2
                    k.dma("y", xg[:, sl, :],
                          x_in[t0 + sub * 128:t0 + (sub + 1) * 128, :],
                          writes=[("xg", sl)])
                    k.op("v", lambda e, sl=sl: e.tensor_copy(
                        out=xgh[:, sl, :], in_=xg[:, sl, :]),
                        reads=[("xg", sl)], writes=[("xgh", sl)])
                    k.op("v", lambda e, sl=sl: e.tensor_tensor(
                        out=xgl[:, sl, :], in0=xg[:, sl, :],
                        in1=xgh[:, sl, :], op=ALU.subtract),
                        reads=[("xg", sl), ("xgh", sl)], writes=[("xgl", sl)])
                    for dc in range(D // 128):
                        for src_t, srcb, dst_t, dstb in (
                                (xgh, ("xgh", sl), xt_h, ("xt_h", dc)),
                                (xgl, ("xgl", sl), xt_l, ("xt_l", dc))):
                            bank, bkey = ((pT, "pT0") if jrot % 2 == 0
                                          else (pT2, "pT1"))
                            jrot += 1
                            k.mm(bank[:, 0:128],
                                 src_t[:, sl, dc * 128:(dc + 1) * 128],
                                 identb[:], [srcb, "identb"], bkey, True, True)
                            k.op("v", lambda e, bank=bank, dst_t=dst_t, dc=dc,
                                 sub=sub: e.tensor_copy(
                                     out=dst_t[:, dc, sub * 128:(sub + 1) * 128],
                                     in_=bank[:, 0:128]),
                                 reads=[bkey], writes=[dstb])

                for cg in range(2):
                    k.dma("y", wa_h[:, :, :],
                          s_wkvc[0][:, cg * 256:(cg + 1) * 256].rearrange(
                              "(a p) c -> p a c", p=128), writes=["wa_h"])
                    k.dma("y", wa_l[:, :, :],
                          s_wkvc[1][:, cg * 256:(cg + 1) * 256].rearrange(
                              "(a p) c -> p a c", p=128), writes=["wa_l"])
                    for cfi in range(2):
                        cf = cg * 2 + cfi
                        for d in range(16):
                            lhh = wa_h[:, d, cfi * 128:(cfi + 1) * 128]
                            lhl = wa_l[:, d, cfi * 128:(cfi + 1) * 128]
                            k.mm(pA[:, 0:TC], lhh, xt_h[:, d, :], ["wa_h"] + xt_reads,
                                 "pA", d == 0, False)
                            k.mm(pA[:, 0:TC], lhh, xt_l[:, d, :], ["wa_h"] + xt_reads,
                                 "pA", False, False)
                            k.mm(pA[:, 0:TC], lhl, xt_h[:, d, :], ["wa_l"] + xt_reads,
                                 "pA", False, d == 15)
                        split_evac(pA[:, 0:TC], "pA", ct_h[:, cf, :], ct_l[:, cf, :],
                                   ("ct_h", cf), ("ct_l", cf))

                k.dma("y", wb_h[:, :, :],
                      s_wkdec[0].rearrange("(a p) c -> p a c", p=128),
                      writes=["wb_h"])
                k.dma("y", wb_l[:, :, :],
                      s_wkdec[1].rearrange("(a p) c -> p a c", p=128),
                      writes=["wb_l"])
                for kv in range(KVH):
                    for cf in range(4):
                        lhh = wb_h[:, cf, kv * 128:(kv + 1) * 128]
                        lhl = wb_l[:, cf, kv * 128:(kv + 1) * 128]
                        k.mm(pA[:, 0:TC], lhh, ct_h[:, cf, :], ["wb_h"] + ct_reads,
                             "pA", cf == 0, False)
                        k.mm(pA[:, 0:TC], lhh, ct_l[:, cf, :], ["wb_h"] + ct_reads,
                             "pA", False, False)
                        k.mm(pA[:, 0:TC], lhl, ct_h[:, cf, :], ["wb_l"] + ct_reads,
                             "pA", False, cf == 3)
                    split_evac(pA[:, 0:TC], "pA",
                               kn_sb[0][:, kv, t0:t0 + TC],
                               kn_sb[1][:, kv, t0:t0 + TC],
                               ("kn", kv), ("knl", kv))

                k.dma("y", wb_h[:, :, :],
                      s_wvdec[0].rearrange("(a p) c -> p a c", p=128),
                      writes=["wb_h"])
                k.dma("y", wb_l[:, :, :],
                      s_wvdec[1].rearrange("(a p) c -> p a c", p=128),
                      writes=["wb_l"])
                for sub in range(TC // 128):
                    kidx = (t0 + sub * 128) // 128
                    for cf in range(4):
                        lhh = ct_h[:, cf, sub * 128:(sub + 1) * 128]
                        lhl = ct_l[:, cf, sub * 128:(sub + 1) * 128]
                        k.mm(pB[:, 0:512], lhh, wb_h[:, cf, :], ["wb_h"] + ct_reads,
                             "pB", cf == 0, False)
                        k.mm(pB[:, 0:512], lhl, wb_h[:, cf, :], ["wb_h"] + ct_reads,
                             "pB", False, False)
                        k.mm(pB[:, 0:512], lhh, wb_l[:, cf, :], ["wb_l"] + ct_reads,
                             "pB", False, cf == 3)
                    split_evac(pB[:, 0:512], "pB",
                               v_sb[0][:, kidx, :], v_sb[1][:, kidx, :],
                               ("v", kidx), ("vl", kidx))

                k.dma("y", wb_h[:, :, 0:256],
                      s_wkr[0].rearrange("(a p) c -> p a c", p=128),
                      writes=["wb_h"])
                k.dma("y", wb_l[:, :, 0:256],
                      s_wkr[1].rearrange("(a p) c -> p a c", p=128),
                      writes=["wb_l"])
                for blk in range(KVH // 2):
                    for cf in range(4):
                        lhh = wb_h[:, cf, blk * 128:(blk + 1) * 128]
                        lhl = wb_l[:, cf, blk * 128:(blk + 1) * 128]
                        k.mm(pA[:, 0:TC], lhh, ct_h[:, cf, :], ["wb_h"] + ct_reads,
                             "pA", cf == 0, False)
                        k.mm(pA[:, 0:TC], lhh, ct_l[:, cf, :], ["wb_h"] + ct_reads,
                             "pA", False, False)
                        k.mm(pA[:, 0:TC], lhl, ct_h[:, cf, :], ["wb_l"] + ct_reads,
                             "pA", False, cf == 3)
                    split_evac(pA[:, 0:TC], "pA",
                               kr_sb[0][:, blk, t0:t0 + TC],
                               kr_sb[1][:, blk, t0:t0 + TC],
                               ("kr", blk), ("krl", blk))

                for h in range(H):
                    if h % 2 == 0:
                        k.dma("y", wa_h[:, :, :],
                              s_wq[0][:, h * HD:(h + 2) * HD].rearrange(
                                  "(a p) c -> p a c", p=128), writes=["wa_h"])
                        k.dma("y", wa_l[:, :, :],
                              s_wq[1][:, h * HD:(h + 2) * HD].rearrange(
                                  "(a p) c -> p a c", p=128), writes=["wa_l"])
                    hc = (h % 2) * 128
                    for d in range(16):
                        lhh = wa_h[:, d, hc:hc + 128]
                        lhl = wa_l[:, d, hc:hc + 128]
                        k.mm(pA[:, 0:TC], lhh, xt_h[:, d, :], ["wa_h"] + xt_reads,
                             "pA", d == 0, False)
                        k.mm(pA[:, 0:TC], lhh, xt_l[:, d, :], ["wa_h"] + xt_reads,
                             "pA", False, False)
                        k.mm(pA[:, 0:TC], lhl, xt_h[:, d, :], ["wa_l"] + xt_reads,
                             "pA", False, d == 15)
                    split_evac(pA[:, 0:TC], "pA", ev_h[:, 0:TC], ev_l[:, 0:TC],
                               "ev_h", "ev_l")
                    k.dma("g", s_qn[0][h * 128:(h + 1) * 128, t0:t0 + TC],
                          ev_h[:, 0:TC], reads=["ev_h"], writes=["scr_qn0"])
                    k.dma("g", s_qn[1][h * 128:(h + 1) * 128, t0:t0 + TC],
                          ev_l[:, 0:TC], reads=["ev_l"], writes=["scr_qn1"])

                for blk in range(H // 2):
                    if blk % 2 == 0:
                        k.dma("y", wa_h[:, :, :],
                              s_wqr[0][:, blk * 128:(blk + 2) * 128].rearrange(
                                  "(a p) c -> p a c", p=128), writes=["wa_h"])
                        k.dma("y", wa_l[:, :, :],
                              s_wqr[1][:, blk * 128:(blk + 2) * 128].rearrange(
                                  "(a p) c -> p a c", p=128), writes=["wa_l"])
                    hc = (blk % 2) * 128
                    for d in range(16):
                        lhh = wa_h[:, d, hc:hc + 128]
                        lhl = wa_l[:, d, hc:hc + 128]
                        k.mm(pA[:, 0:TC], lhh, xt_h[:, d, :], ["wa_h"] + xt_reads,
                             "pA", d == 0, False)
                        k.mm(pA[:, 0:TC], lhh, xt_l[:, d, :], ["wa_h"] + xt_reads,
                             "pA", False, False)
                        k.mm(pA[:, 0:TC], lhl, xt_h[:, d, :], ["wa_l"] + xt_reads,
                             "pA", False, d == 15)
                    split_evac(pA[:, 0:TC], "pA", ev_h[:, 0:TC], ev_l[:, 0:TC],
                               "ev_h", "ev_l")
                    k.dma("g", s_qr[0][blk * 128:(blk + 1) * 128, t0:t0 + TC],
                          ev_h[:, 0:TC], reads=["ev_h"], writes=["scr_qr0"])
                    k.dma("g", s_qr[1][blk * 128:(blk + 1) * 128, t0:t0 + TC],
                          ev_l[:, 0:TC], reads=["ev_l"], writes=["scr_qr1"])
            ctx.close()

        phase_p0()
        k.new_phase()

        def phase_p1():
            ctx = contextlib.ExitStack()
            C = ctx.enter_context
            qn_h = C(nc.sbuf_tensor("qn_h", [128, S], BF16))
            qn_l = C(nc.sbuf_tensor("qn_l", [128, S], BF16))
            qr_h = C(nc.sbuf_tensor("qr_h", [128, S], BF16))
            qr_l = C(nc.sbuf_tensor("qr_l", [128, S], BF16))
            eP = C(nc.sbuf_tensor("eP", [128, 2, 512], F32))
            n_h = C(nc.sbuf_tensor("n_h", [128, 3, 512], BF16))
            n_l = C(nc.sbuf_tensor("n_l", [128, 3, 512], BF16))
            ones1 = C(nc.sbuf_tensor("ones1", [128, 1], BF16))
            zrow = C(nc.sbuf_tensor("zrow", [128, 512], F32))
            rbs = C(nc.sbuf_tensor("rbs", [128, 512], F32))
            otmp = C(nc.sbuf_tensor("otmp", [128, 512], F32))
            ot_hb = C(nc.sbuf_tensor("ot_hb", [128, 512], BF16))
            ot_lb = C(nc.sbuf_tensor("ot_lb", [128, 512], BF16))
            pS2 = C(nc.psum_tensor("pS2", [128, 1024], F32))
            pM = C(nc.psum_tensor("pM", [128, 512], F32))
            pO = C(nc.psum_tensor("pO", [128, 512], F32))
            pR = C(nc.psum_tensor("pR", [128, 512], F32))
            dmask = C(nc.sbuf_tensor("sb_dmask", [128, 4, 512], F32))
            k.dma("y", dmask[:], dmask_in.rearrange("(a p) c -> p a c", p=128),
                  writes=["dmask"])

            k.op("v", lambda e: e.memset(zrow[:], 0.0), writes=["zrow"])
            k.op("v", lambda e: e.memset(ones1[:, 0:1], 1.0), writes=["ones1"])

            NQT = S // 512
            for h in range(H):
                kv = h // (H // KVH)
                kvb = kv // 2
                kvs = (kv % 2) * 64
                k.dma("y", qn_h[:], s_qn[0][h * 128:(h + 1) * 128, :],
                      writes=["qn_h"])
                k.dma("y", qn_l[:], s_qn[1][h * 128:(h + 1) * 128, :],
                      writes=["qn_l"])
                qrb = (h // 8) * 4 + (h % 4)
                k.dma("y", qr_h[kvs:kvs + 64, :],
                      s_qr[0][qrb * 128 + kvs:qrb * 128 + kvs + 64, :],
                      writes=["qr_h"])
                k.dma("y", qr_l[kvs:kvs + 64, :],
                      s_qr[1][qrb * 128 + kvs:qrb * 128 + kvs + 64, :],
                      writes=["qr_l"])
                for qt in range(NQT):
                    q0 = qt * 512
                    nki = 4 * qt + 4
                    rq_nh = qn_h[:, q0:q0 + 512]
                    rq_nl = qn_l[:, q0:q0 + 512]
                    rq_rh = qr_h[kvs:kvs + 64, q0:q0 + 512]
                    rq_rl = qr_l[kvs:kvs + 64, q0:q0 + 512]
                    def emit_qk(ki):
                        sbank = ki % 2
                        sap = pS2[:, sbank * 512:(sbank + 1) * 512]
                        sbk = ("pS2", sbank)
                        kn0 = kn_sb[0][:, kv, ki * 128:(ki + 1) * 128]
                        kn1 = kn_sb[1][:, kv, ki * 128:(ki + 1) * 128]
                        kr0 = kr_sb[0][kvs:kvs + 64, kvb, ki * 128:(ki + 1) * 128]
                        kr1 = kr_sb[1][kvs:kvs + 64, kvb, ki * 128:(ki + 1) * 128]
                        knb = ("kn", kv)
                        knlb = ("knl", kv)
                        krb = ("kr", kvb)
                        krlb = ("krl", kvb)
                        k.mm(sap, kn0, rq_nh, [knb, "qn_h"], sbk, True, False)
                        k.mm(sap, kn0, rq_nl, [knb, "qn_l"], sbk, False, False)
                        k.mm(sap, kn1, rq_nh, [knlb, "qn_h"], sbk, False, False)
                        k.mm(sap, kr0, rq_rh, [krb, "qr_h"], sbk, False, False)
                        k.mm(sap, kr0, rq_rl, [krb, "qr_l"], sbk, False, False)
                        k.mm(sap, kr1, rq_rh, [krlb, "qr_h"], sbk, False, True)
                        eslot = ki % 2
                        ev = eP[:, eslot, :]
                        ebk = ("eP", eslot)
                        k.op("s", lambda e, sap=sap, ev=ev: e.activation(
                            out=ev, in_=sap, func=ACT.Exp, scale=EXP_SCALE),
                            reads=[sbk], writes=[ebk])
                        j = ki - 4 * qt
                        if j >= 0:
                            k.op("v", lambda e, ev=ev, j=j: e.tensor_tensor(
                                out=ev, in0=ev, in1=dmask[:, j, :], op=ALU.mult),
                                reads=[ebk, "dmask"], writes=[ebk])
                        nslot = ki % 3
                        nhv = n_h[:, nslot, :]
                        nlv = n_l[:, nslot, :]
                        nhb = ("n_h", nslot)
                        nlb = ("n_l", nslot)
                        k.op("s", lambda e, ev=ev, nhv=nhv: e.copy(
                            out=nhv, in_=ev), reads=[ebk], writes=[nhb])
                        k.op("v", lambda e, ev=ev, nhv=nhv, nlv=nlv: e.tensor_tensor(
                            out=nlv, in0=ev, in1=nhv, op=ALU.subtract),
                            reads=[ebk, nhb], writes=[nlb])

                    def emit_pv(ki):
                        nslot = ki % 3
                        nhv = n_h[:, nslot, :]
                        nlv = n_l[:, nslot, :]
                        nhb = ("n_h", nslot)
                        nlb = ("n_l", nslot)
                        k.mm(pM[0:1, 0:512], ones1[:, 0:1], nhv,
                             ["ones1", nhb], "pM", ki == 0, False)
                        k.mm(pM[0:1, 0:512], ones1[:, 0:1], nlv,
                             ["ones1", nlb], "pM", False, ki == nki - 1)
                        vh = v_sb[0][:, ki, kv * 128:(kv + 1) * 128]
                        vl = v_sb[1][:, ki, kv * 128:(kv + 1) * 128]
                        k.mm(pO[:, 0:512], vh, nhv, [("v", ki), nhb], "pO",
                             ki == 0, False)
                        k.mm(pO[:, 0:512], vl, nhv, [("vl", ki), nhb], "pO",
                             False, False)
                        k.mm(pO[:, 0:512], vh, nlv, [("v", ki), nlb], "pO",
                             False, ki == nki - 1, inc=True)

                    emit_qk(0)
                    for ki in range(1, nki):
                        emit_qk(ki)
                        emit_pv(ki - 1)
                    emit_pv(nki - 1)
                    k.op("v", lambda e: e.reciprocal(
                        out=zrow[0:1, 0:512], in_=pM[0:1, 0:512]),
                        reads=["pM"], writes=["zrow"])
                    k.mm(pR[:, 0:512], ones[:], zrow[:, 0:512],
                         ["ones", "zrow"], "pR", True, True)
                    k.op("v", lambda e: e.tensor_copy(
                        out=rbs[:, 0:512], in_=pR[:, 0:512]),
                        reads=["pR"], writes=["rbs"])
                    k.op("v", lambda e: e.tensor_tensor(
                        out=otmp[:, 0:512], in0=pO[:, 0:512], in1=rbs[:, 0:512],
                        op=ALU.mult), reads=["pO", "rbs"], writes=["otmp"])
                    k.op("v", lambda e: e.tensor_copy(
                        out=ot_hb[:, 0:512], in_=otmp[:, 0:512]),
                        reads=["otmp"], writes=["ot_hb"])
                    k.op("v", lambda e: e.tensor_tensor(
                        out=ot_lb[:, 0:512], in0=otmp[:, 0:512],
                        in1=ot_hb[:, 0:512], op=ALU.subtract),
                        reads=["otmp", "ot_hb"], writes=["ot_lb"])
                    k.dma("g", s_ot[0][h * 128:(h + 1) * 128, q0:q0 + 512],
                          ot_hb[:, 0:512], reads=["ot_hb"], writes=["scr_ot0"])
                    k.dma("g", s_ot[1][h * 128:(h + 1) * 128, q0:q0 + 512],
                          ot_lb[:, 0:512], reads=["ot_lb"], writes=["scr_ot1"])
            ctx.close()

        phase_p1()
        ctx01.close()
        k.new_phase()

        s1inv = E(nc.sbuf_tensor("s1inv", [128, NSUB], F32))
        c2s = E(nc.sbuf_tensor("c2s", [128, NSUB], F32))
        is2 = E(nc.sbuf_tensor("is2", [128, NSUB], F32))
        ctx2ab = contextlib.ExitStack()
        xqT = ctx2ab.enter_context(nc.sbuf_tensor("xqT", [128, D // 128, S], BF16))
        xq_reads = [("xqT", dc) for dc in range(D // 128)]

        def phase_p2a():
            ctx = contextlib.ExitStack()
            C = ctx.enter_context
            wo2 = [C(nc.sbuf_tensor(f"wo2_{p}", [128, 16, 512], BF16))
                   for p in "hl"]
            ot2 = [C(nc.sbuf_tensor(f"ot2_{p}", [128, 16, 128], BF16))
                   for p in "hl"]
            xgh2 = C(nc.sbuf_tensor("xgh2", [128, 512], F32))
            x2g = C(nc.sbuf_tensor("x2g", [128, D], F32))
            xh0 = C(nc.sbuf_tensor("xh0", [128, 512], F32))
            lnm = C(nc.sbuf_tensor("lnm", [128, 1], F32))
            lnv = C(nc.sbuf_tensor("lnv", [128, 1], F32))
            zq = C(nc.sbuf_tensor("zq", [128, D], F32))
            cq = C(nc.sbuf_tensor("cq", [128, D], F32))
            qiq = C(nc.sbuf_tensor("qiq", [128, D], BF16))
            pZ = C(nc.psum_tensor("pZ", [128, 512], F32))
            pQT = C(nc.psum_tensor("pQT", [128, 512], F32))
            pQT2 = C(nc.psum_tensor("pQT2", [128, 512], F32))

            for half in range(4):
                c0 = half * 512
                for p in range(2):
                    k.dma("y", wo2[p][:, :, :],
                          s_wo[p][:, c0:c0 + 512].rearrange(
                              "(a p) c -> p a c", p=128),
                          writes=[("wo2", p)])
                for sub in range(NSUB):
                    t0 = sub * 128
                    for p in range(2):
                        k.dma("y", ot2[p][:, :, :],
                              s_ot[p][:, t0:t0 + 128].rearrange(
                                  "(a p) c -> p a c", p=128),
                              writes=[("ot2", p)])
                    k.dma("y", xgh2[:, :], x_in[t0:t0 + 128, c0:c0 + 512],
                          writes=["xgh2"])
                    for h in range(H):
                        combos = [(ot2[0], ("ot2", 0), wo2[0], ("wo2", 0)),
                                  (ot2[0], ("ot2", 0), wo2[1], ("wo2", 1)),
                                  (ot2[1], ("ot2", 1), wo2[0], ("wo2", 0))]
                        for pi, (lh, lhb, wt, wtb) in enumerate(combos):
                            k.mm(pZ[:, 0:512], lh[:, h, :], wt[:, h, :],
                                 [lhb, wtb], "pZ",
                                 start=(h == 0 and pi == 0),
                                 stop=(h == H - 1 and pi == 2),
                                 inc=(pi == 2))
                    k.op("v", lambda e, c0=c0: e.tensor_tensor(
                        out=x2g[:, c0:c0 + 512], in0=pZ[:, 0:512],
                        in1=xgh2[:, :], op=ALU.add),
                        reads=["pZ", "xgh2"], writes=["x2g"])
                    if half < 3:
                        k.dma("g", s_x2[t0:t0 + 128, c0:c0 + 512],
                              x2g[:, c0:c0 + 512], reads=["x2g"],
                              writes=["scr_x2"])
                        continue
                    for qq in range(3):
                        k.dma("y", xh0[:, :], s_x2[t0:t0 + 128,
                                                   qq * 512:(qq + 1) * 512],
                              reads=["scr_x2"], writes=["xh0"])
                        k.op("v", lambda e, qq=qq: e.tensor_copy(
                            out=x2g[:, qq * 512:(qq + 1) * 512], in_=xh0[:, :]),
                            reads=["xh0"], writes=["x2g"])
                    k.op("v", lambda e: e.reduce_sum(
                        out=lnm[:, 0:1], in_=x2g[:, :], axis=AX.X),
                        reads=["x2g"], writes=["lnm"])
                    k.op("v", lambda e: e.tensor_scalar(
                        out=lnm[:, 0:1], in0=lnm[:, 0:1], scalar1=1.0 / D,
                        scalar2=None, op0=ALU.mult), reads=["lnm"], writes=["lnm"])
                    k.op("v", lambda e: e.tensor_scalar(
                        out=x2g[:, :], in0=x2g[:, :], scalar1=lnm[:, 0:1],
                        scalar2=None, op0=ALU.subtract),
                        reads=["x2g", "lnm"], writes=["x2g"])
                    k.op("s", lambda e: e.square(out=zq[:, :], in_=x2g[:, :]),
                         reads=["x2g"], writes=["zq"])
                    k.op("v", lambda e: e.reduce_sum(
                        out=lnv[:, 0:1], in_=zq[:, :], axis=AX.X),
                        reads=["zq"], writes=["lnv"])
                    k.op("s", lambda e: e.activation(
                        out=lnv[:, 0:1], in_=lnv[:, 0:1], func=ACT.Sqrt,
                        bias=ceps[:, 0:1], scale=1.0 / D),
                        reads=["lnv", "ceps"], writes=["lnv"])
                    k.op("v", lambda e: e.reciprocal(
                        out=lnv[:, 0:1], in_=lnv[:, 0:1]),
                        reads=["lnv"], writes=["lnv"])
                    k.op("v", lambda e: e.tensor_scalar(
                        out=x2g[:, :], in0=x2g[:, :], scalar1=lnv[:, 0:1],
                        scalar2=None, op0=ALU.mult),
                        reads=["x2g", "lnv"], writes=["x2g"])
                    k.dma("g", s_x2[t0:t0 + 128, :], x2g[:, :],
                          reads=["x2g"], writes=["scr_x2b"])
                    k.op("v", lambda e: e.reduce_max(
                        out=lnm[:, 0:1], in_=x2g[:, :], axis=AX.X,
                        apply_absolute_value=True), reads=["x2g"], writes=["lnm"])
                    k.op("v", lambda e: e.tensor_scalar(
                        out=lnm[:, 0:1], in0=lnm[:, 0:1], scalar1=1e-5,
                        scalar2=None, op0=ALU.max), reads=["lnm"], writes=["lnm"])
                    k.op("v", lambda e, sub=sub: e.tensor_scalar(
                        out=s1inv[:, sub:sub + 1], in0=lnm[:, 0:1],
                        scalar1=1.0 / 127.0, scalar2=None, op0=ALU.mult),
                        reads=["lnm"], writes=["s1inv"])
                    k.op("v", lambda e: e.reciprocal(
                        out=lnv[:, 0:1], in_=lnm[:, 0:1]),
                        reads=["lnm"], writes=["lnv"])
                    k.op("v", lambda e: e.tensor_scalar(
                        out=lnv[:, 0:1], in0=lnv[:, 0:1], scalar1=127.0,
                        scalar2=None, op0=ALU.mult), reads=["lnv"], writes=["lnv"])
                    k.op("v", lambda e: e.tensor_scalar(
                        out=zq[:, :], in0=x2g[:, :], scalar1=lnv[:, 0:1],
                        scalar2=128.5, op0=ALU.mult, op1=ALU.add),
                        reads=["x2g", "lnv"], writes=["zq"])
                    for kb in range(7, -1, -1):
                        t = float(1 << kb)
                        k.op("v", lambda e, t=t: e.tensor_single_scalar(
                            out=cq[:, :], in_=zq[:, :], scalar=t, op=ALU.is_ge),
                            reads=["zq"], writes=["cq"])
                        k.op("v", lambda e, t=t: e.scalar_tensor_tensor(
                            out=zq[:, :], in0=cq[:, :], scalar=-t, in1=zq[:, :],
                            op0=ALU.mult, op1=ALU.add),
                            reads=["cq", "zq"], writes=["zq"])
                    k.op("v", lambda e: e.tensor_scalar(
                        out=cq[:, :], in0=x2g[:, :], scalar1=lnv[:, 0:1],
                        scalar2=0.5, op0=ALU.mult, op1=ALU.add),
                        reads=["x2g", "lnv"], writes=["cq"])
                    k.op("v", lambda e: e.tensor_tensor(
                        out=qiq[:, :], in0=cq[:, :], in1=zq[:, :],
                        op=ALU.subtract), reads=["cq", "zq"], writes=["qiq"])
                    for dc in range(D // 128):
                        bank, bkey = ((pQT, "pQT0") if dc % 2 == 0
                                      else (pQT2, "pQT1"))
                        k.mm(bank[:, 0:128], qiq[:, dc * 128:(dc + 1) * 128],
                             identb[:], ["qiq", "identb"], bkey, True, True)
                        k.op("s", lambda e, bank=bank, dc=dc, sub=sub:
                             e.copy(
                                 out=xqT[:, dc, sub * 128:(sub + 1) * 128],
                                 in_=bank[:, 0:128]),
                             reads=[bkey], writes=[("xqT", dc)])
            ctx.close()

        if upto >= 3:
            phase_p2a()
        k.new_phase()

        def phase_p2b():
            ctx = contextlib.ExitStack()
            C = ctx.enter_context
            w13f = C(nc.sbuf_tensor("w13f", [128, 16, 512], F32))
            w13h = C(nc.sbuf_tensor("w13h", [128, 16, 512], BF16))
            w13l = C(nc.sbuf_tensor("w13l", [128, 16, 512], BF16))
            msac = C(nc.sbuf_tensor("msac", [128, NSUB], F32))
            mxac = C(nc.sbuf_tensor("mxac", [128, NSUB], F32))
            rg1 = C(nc.sbuf_tensor("rg1", [128, 256], F32))
            rg2 = C(nc.sbuf_tensor("rg2", [128, 256], F32))
            rg3 = C(nc.sbuf_tensor("rg3", [128, 256], F32))
            red1 = C(nc.sbuf_tensor("red1", [128, 1], F32))
            nwbs = C(nc.sbuf_tensor("nwbs", [128, FH], F32))
            pF = C(nc.psum_tensor("pF", [128, 1024], F32))
            if True:
                w13ff = w13f.rearrange("p a b -> p (a b)")
                w13hf = w13h.rearrange("p a b -> p (a b)")
                w13lf = w13l.rearrange("p a b -> p (a b)")
                k.dma("y", nwbs[:], nwb_in, writes=["nwbs"])
                k.op("v", lambda e: e.memset(msac[:], 0.0), writes=["msac"])
                k.op("v", lambda e: e.memset(mxac[:], 0.0), writes=["mxac"])

                for fc in range(NFC):
                    k.dma("y", w13f[:, :, :],
                          w13t_in[:, fc * 512:(fc + 1) * 512].rearrange(
                              "(a p) c -> p a c", p=128), writes=["w13f"])
                    k.op("s", lambda e: e.copy(out=w13hf, in_=w13ff),
                         reads=["w13f"], writes=["w13h"])
                    k.op("v", lambda e: e.tensor_tensor(
                        out=w13lf, in0=w13ff, in1=w13hf, op=ALU.subtract),
                        reads=["w13f", "w13h"], writes=["w13l"])
                    if P2BCUT < 1:
                        continue
                    for sub in range(NSUB):
                        bank = sub % 2
                        for pi, (wt, wtb) in enumerate([(w13h, "w13h"),
                                                        (w13l, "w13l")]):
                            for d in range(16):
                                k.mm(pF[:, bank * 512:bank * 512 + 512],
                                     xqT[:, d, sub * 128:(sub + 1) * 128],
                                     wt[:, d, :], xq_reads + [wtb], ("pF", bank),
                                     start=(pi == 0 and d == 0),
                                     stop=(pi == 1 and d == 15))
                        if P2BCUT < 2:
                            continue
                        x1p = pF[:, bank * 512:bank * 512 + 256]
                        x3p = pF[:, bank * 512 + 256:bank * 512 + 512]
                        k.op("s", lambda e, x1p=x1p, sub=sub: e.activation(
                            out=rg1[:, :], in_=x1p, func=ACT.Relu,
                            scale=s1inv[:, sub:sub + 1]),
                            reads=[("pF", bank), "s1inv"], writes=["rg1"])
                        if P2BCUT < 21:
                            continue
                        k.op("s", lambda e: e.square(out=rg2[:, :], in_=rg1[:, :]),
                             reads=["rg1"], writes=["rg2"])
                        if P2BCUT < 22:
                            continue
                        k.op("s", lambda e, x3p=x3p: e.copy(
                            out=rg3[:, :], in_=x3p),
                            reads=[("pF", bank)], writes=["rg3"])
                        if P2BCUT < 23:
                            continue
                        k.op("v", lambda e, sub=sub: e.scalar_tensor_tensor(
                            out=rg1[:, :], in0=rg2[:, :],
                            scalar=s1inv[:, sub:sub + 1], in1=rg3[:, :],
                            op0=ALU.mult, op1=ALU.mult),
                            reads=["rg2", "rg3", "s1inv"], writes=["rg1"])
                        if P2BCUT < 24:
                            continue

                        k.op("s", lambda e: e.square(out=rg2[:, :], in_=rg1[:, :]),
                             reads=["rg1"], writes=["rg2"])
                        k.op("v", lambda e: e.reduce_sum(
                            out=red1[:, 0:1], in_=rg2[:, :], axis=AX.X),
                            reads=["rg2"], writes=["red1"])
                        k.op("v", lambda e, sub=sub: e.tensor_tensor(
                            out=msac[:, sub:sub + 1], in0=msac[:, sub:sub + 1],
                            in1=red1[:, 0:1], op=ALU.add),
                            reads=["msac", "red1"], writes=["msac"])
                        k.op("v", lambda e, fc=fc: e.tensor_tensor(
                            out=rg3[:, :], in0=rg1[:, :],
                            in1=nwbs[:, fc * 256:(fc + 1) * 256], op=ALU.mult),
                            reads=["rg1", "nwbs"], writes=["rg3"])
                        k.op("v", lambda e: e.reduce_max(
                            out=red1[:, 0:1], in_=rg3[:, :], axis=AX.X,
                            apply_absolute_value=True),
                            reads=["rg3"], writes=["red1"])
                        k.op("v", lambda e, sub=sub: e.tensor_tensor(
                            out=mxac[:, sub:sub + 1], in0=mxac[:, sub:sub + 1],
                            in1=red1[:, 0:1], op=ALU.max),
                            reads=["mxac", "red1"], writes=["mxac"])
                        if P2BCUT >= 4:
                            k.dma("g", s_gnw[sub * 128:(sub + 1) * 128,
                                            fc * 256:(fc + 1) * 256],
                                  rg3[:, :], reads=["rg3"], writes=["scr_gnw"])

                for sub in range(NSUB):
                    k.op("s", lambda e, sub=sub: e.activation(
                        out=c2s[:, sub:sub + 1], in_=msac[:, sub:sub + 1],
                        func=ACT.Sqrt, bias=ceps[:, 0:1], scale=1.0 / FH),
                        reads=["msac", "ceps"], writes=[("c2s", sub)])
                    k.op("v", lambda e, sub=sub: e.reciprocal(
                        out=c2s[:, sub:sub + 1], in_=c2s[:, sub:sub + 1]),
                        reads=[("c2s", sub)], writes=[("c2s", sub)])
                    k.op("v", lambda e, sub=sub: e.tensor_tensor(
                        out=is2[:, sub:sub + 1], in0=mxac[:, sub:sub + 1],
                        in1=c2s[:, sub:sub + 1], op=ALU.mult),
                        reads=["mxac", ("c2s", sub)], writes=[("is2", sub)])
                    k.op("v", lambda e, sub=sub: e.tensor_scalar(
                        out=is2[:, sub:sub + 1], in0=is2[:, sub:sub + 1],
                        scalar1=1e-5, scalar2=None, op0=ALU.max),
                        reads=[("is2", sub)], writes=[("is2", sub)])
                    k.op("v", lambda e, sub=sub: e.reciprocal(
                        out=red1[:, 0:1], in_=is2[:, sub:sub + 1]),
                        reads=[("is2", sub)], writes=["red1"])
                    k.op("v", lambda e: e.tensor_scalar(
                        out=red1[:, 0:1], in0=red1[:, 0:1], scalar1=127.0,
                        scalar2=None, op0=ALU.mult), reads=["red1"], writes=["red1"])
                    k.op("v", lambda e, sub=sub: e.tensor_tensor(
                        out=c2s[:, sub:sub + 1], in0=red1[:, 0:1],
                        in1=c2s[:, sub:sub + 1], op=ALU.mult),
                        reads=["red1", ("c2s", sub)], writes=[("c2s", sub)])
                    k.op("v", lambda e, sub=sub: e.tensor_scalar(
                        out=is2[:, sub:sub + 1], in0=is2[:, sub:sub + 1],
                        scalar1=1.0 / 127.0, scalar2=None, op0=ALU.mult),
                        reads=[("is2", sub)], writes=[("is2", sub)])
            ctx.close()

        if upto >= 4:
            phase_p2b()
        ctx2ab.close()
        k.new_phase()

        def phase_p2c():
            ctx = contextlib.ExitStack()
            C = ctx.enter_context
            gbuf = C(nc.sbuf_tensor("gbuf", [128, 512], F32))
            zq2 = C(nc.sbuf_tensor("zq2", [128, 512], F32))
            cq2 = C(nc.sbuf_tensor("cq2", [128, 512], F32))
            qi2 = C(nc.sbuf_tensor("qi2", [128, 512], BF16))
            iqT = C(nc.sbuf_tensor("iqT", [128, NFH, 1024], BF16))
            w2f = [C(nc.sbuf_tensor(f"w2f{i}", [128, 512], F32)) for i in range(2)]
            w2h = [C(nc.sbuf_tensor(f"w2h{i}", [128, 512], BF16)) for i in range(2)]
            w2l = [C(nc.sbuf_tensor(f"w2l{i}", [128, 512], BF16)) for i in range(2)]
            x2b = C(nc.sbuf_tensor("x2b", [128, 512], F32))
            y2 = C(nc.sbuf_tensor("y2", [128, 32, 512], F32))
            lnm2 = C(nc.sbuf_tensor("lnm2", [128, 1], F32))
            lnv2 = C(nc.sbuf_tensor("lnv2", [128, 1], F32))
            sqt2 = C(nc.sbuf_tensor("sqt2", [128, D], F32))
            pG = C(nc.psum_tensor("pG", [128, 4096], F32))
            if True:
                SGN = min(8, NSUB)
                for sg in range(NSUB // SGN):
                    subs = [sg * SGN + i for i in range(SGN)]
                    for si, sub in enumerate(subs):
                        for f0 in range(0, FH, 512):
                            k.dma("y", gbuf[:, :],
                                  s_gnw[sub * 128:(sub + 1) * 128, f0:f0 + 512],
                                  writes=["gbuf"])
                            k.op("v", lambda e, sub=sub: e.tensor_scalar(
                                out=zq2[:, :], in0=gbuf[:, :],
                                scalar1=c2s[:, sub:sub + 1], scalar2=128.5,
                                op0=ALU.mult, op1=ALU.add),
                                reads=["gbuf", ("c2s", sub)], writes=["zq2"])
                            for kb in range(7, -1, -1):
                                t = float(1 << kb)
                                k.op("v", lambda e, t=t: e.tensor_single_scalar(
                                    out=cq2[:, :], in_=zq2[:, :], scalar=t,
                                    op=ALU.is_ge), reads=["zq2"], writes=["cq2"])
                                k.op("v", lambda e, t=t: e.scalar_tensor_tensor(
                                    out=zq2[:, :], in0=cq2[:, :], scalar=-t,
                                    in1=zq2[:, :], op0=ALU.mult, op1=ALU.add),
                                    reads=["cq2", "zq2"], writes=["zq2"])
                            k.op("v", lambda e, sub=sub: e.tensor_scalar(
                                out=cq2[:, :], in0=gbuf[:, :],
                                scalar1=c2s[:, sub:sub + 1], scalar2=0.5,
                                op0=ALU.mult, op1=ALU.add),
                                reads=["gbuf", ("c2s", sub)], writes=["cq2"])
                            k.op("v", lambda e: e.tensor_tensor(
                                out=qi2[:, :], in0=cq2[:, :], in1=zq2[:, :],
                                op=ALU.subtract), reads=["cq2", "zq2"],
                                writes=["qi2"])
                            for j in range(4):
                                fch = f0 // 128 + j
                                bank = j % 2
                                bkey = ("pG", bank)
                                k.mm(pG[:, bank * 512:bank * 512 + 128],
                                     qi2[:, j * 128:(j + 1) * 128],
                                     identb[:], ["qi2", "identb"], bkey,
                                     True, True)
                                k.op("s", lambda e, bank=bank, fch=fch,
                                     si=si: e.copy(
                                         out=iqT[:, fch, si * 128:(si + 1) * 128],
                                         in_=pG[:, bank * 512:bank * 512 + 128]),
                                     reads=[bkey], writes=[("iqT", fch)])
                    iq_reads = [("iqT", f) for f in range(NFH)]
                    for dmc in range(4):
                        for fch in range(NFH):
                            bi = fch % 2
                            q = "y" if fch % 2 == 0 else "g"
                            k.dma(q, w2f[bi][:, :],
                                  w2t_in[fch * 128:(fch + 1) * 128,
                                         dmc * 512:(dmc + 1) * 512],
                                  writes=[("w2f", bi)])
                            k.op("s", lambda e, bi=bi: e.copy(
                                out=w2h[bi][:, :], in_=w2f[bi][:, :]),
                                reads=[("w2f", bi)], writes=[("w2h", bi)])
                            k.op("v", lambda e, bi=bi: e.tensor_tensor(
                                out=w2l[bi][:, :], in0=w2f[bi][:, :],
                                in1=w2h[bi][:, :], op=ALU.subtract),
                                reads=[("w2f", bi), ("w2h", bi)],
                                writes=[("w2l", bi)])
                            for si in range(SGN):
                                k.mm(pG[:, si * 512:(si + 1) * 512],
                                     iqT[:, fch, si * 128:(si + 1) * 128],
                                     w2h[bi][:, :], iq_reads + [("w2h", bi)],
                                     ("pG", si), start=(fch == 0), stop=False)
                                k.mm(pG[:, si * 512:(si + 1) * 512],
                                     iqT[:, fch, si * 128:(si + 1) * 128],
                                     w2l[bi][:, :], iq_reads + [("w2l", bi)],
                                     ("pG", si), False, stop=(fch == NFH - 1),
                                     inc=(si == SGN - 1))
                        for si, sub in enumerate(subs):
                            k.dma("y", x2b[:, :],
                                  s_x2[sub * 128:(sub + 1) * 128,
                                       dmc * 512:(dmc + 1) * 512],
                                  writes=["x2b"])
                            k.op("v", lambda e, si=si, sub=sub, dmc=dmc:
                                 e.scalar_tensor_tensor(
                                     out=y2[:, si * 4 + dmc, :],
                                     in0=pG[:, si * 512:(si + 1) * 512],
                                     scalar=is2[:, sub:sub + 1], in1=x2b[:, :],
                                     op0=ALU.mult, op1=ALU.add),
                                 reads=[("pG", si), ("is2", sub), "x2b"],
                                 writes=[("y2", si, dmc)])
                    for si, sub in enumerate(subs):
                        yv = y2[:, si * 4:(si + 1) * 4, :].rearrange(
                            "p a b -> p (a b)")
                        yr = [("y2", si, d) for d in range(4)]
                        k.op("v", lambda e, yv=yv: e.reduce_sum(
                            out=lnm2[:, 0:1], in_=yv, axis=AX.X),
                            reads=yr, writes=["lnm2"])
                        k.op("v", lambda e: e.tensor_scalar(
                            out=lnm2[:, 0:1], in0=lnm2[:, 0:1], scalar1=1.0 / D,
                            scalar2=None, op0=ALU.mult),
                            reads=["lnm2"], writes=["lnm2"])
                        k.op("v", lambda e, yv=yv: e.tensor_scalar(
                            out=yv, in0=yv, scalar1=lnm2[:, 0:1], scalar2=None,
                            op0=ALU.subtract), reads=yr + ["lnm2"], writes=yr)
                        k.op("s", lambda e, yv=yv: e.square(
                            out=sqt2[:, :], in_=yv), reads=yr, writes=["sqt2"])
                        k.op("v", lambda e: e.reduce_sum(
                            out=lnv2[:, 0:1], in_=sqt2[:, :], axis=AX.X),
                            reads=["sqt2"], writes=["lnv2"])
                        k.op("s", lambda e: e.activation(
                            out=lnv2[:, 0:1], in_=lnv2[:, 0:1], func=ACT.Sqrt,
                            bias=ceps[:, 0:1], scale=1.0 / D),
                            reads=["lnv2", "ceps"], writes=["lnv2"])
                        k.op("v", lambda e: e.reciprocal(
                            out=lnv2[:, 0:1], in_=lnv2[:, 0:1]),
                            reads=["lnv2"], writes=["lnv2"])
                        k.op("v", lambda e, yv=yv: e.tensor_scalar(
                            out=yv, in0=yv, scalar1=lnv2[:, 0:1], scalar2=None,
                            op0=ALU.mult), reads=yr + ["lnv2"], writes=yr)
                        k.dma("g", out_t[sub * 128:(sub + 1) * 128, :], yv,
                              reads=yr, writes=["scr_out"])
            ctx.close()

        if upto >= 5:
            phase_p2c()
        k.wait_all_on("y")

    return nc


def _rope_factor_tables(pairs):
    theta = 10000.0
    freqs = (1.0 / theta ** (np.arange(0, RD, 2, dtype=np.float32)
                             / np.float32(RD))).astype(np.float32)
    t = np.arange(4096, dtype=np.float32)
    f = np.outer(t, freqs).astype(np.float32)
    cos = np.cos(f).astype(np.float32)
    sin = np.sin(f).astype(np.float32)
    m1 = []
    m2 = []
    for ha, hb in pairs:
        for h in (ha, hb):
            m1.append(cos[h]); m1.append(cos[h])
            m2.append(-sin[h]); m2.append(sin[h])
    m1 = np.concatenate(m1).astype(np.float32)[None, :]
    m2 = np.concatenate(m2).astype(np.float32)[None, :]
    return (np.ascontiguousarray(np.broadcast_to(m1, (128, m1.shape[1]))),
            np.ascontiguousarray(np.broadcast_to(m2, (128, m2.shape[1]))))


def _perm_rope_cols(n_heads, pairs=None):
    if pairs is None:
        pairs = [(2 * b, 2 * b + 1) for b in range(n_heads // 2)]
    perm = []
    for ha, hb in pairs:
        for h in (ha, hb):
            perm.extend((h * RD + 2 * np.arange(32)).tolist())
            perm.extend((h * RD + 2 * np.arange(32) + 1).tolist())
    return np.asarray(perm)


_Q_PAIRS = [(0, 4), (1, 5), (2, 6), (3, 7), (8, 12), (9, 13), (10, 14), (11, 15)]


def _diag_masks():
    k = np.arange(128)[:, None]
    q = np.arange(512)[None, :]
    return np.ascontiguousarray(np.concatenate(
        [(q >= k + 128 * j).astype(np.float32) for j in range(4)], axis=0))


def _prep_shared(Wq, Wkvc, Wkdec, Wvdec, Wqrope, Wkrope, Wo, w13, w2, norm_w, S):
    f32 = np.float32
    wq = np.ascontiguousarray(np.asarray(Wq, dtype=f32))
    wqr = np.ascontiguousarray(np.asarray(Wqrope, dtype=f32)[:, _perm_rope_cols(H, _Q_PAIRS)])
    wkvc = np.ascontiguousarray(np.asarray(Wkvc, dtype=f32))
    wkdec = np.ascontiguousarray(np.asarray(Wkdec, dtype=f32))
    wvdec = np.ascontiguousarray(np.asarray(Wvdec, dtype=f32))
    wkr = np.ascontiguousarray(np.asarray(Wkrope, dtype=f32)[:, _perm_rope_cols(KVH)])
    wo_full = np.asarray(Wo, dtype=f32)
    rows = np.concatenate([np.arange(h * (HD + RD), h * (HD + RD) + HD)
                           for h in range(H)])
    wo = np.ascontiguousarray(wo_full[rows, :])
    w13t = np.asarray(w13, dtype=f32).T
    blocks = []
    for j in range(FH // 256):
        blocks.append(w13t[:, j * 256:(j + 1) * 256])
        blocks.append(w13t[:, FH + j * 256:FH + (j + 1) * 256])
    w13t_i = np.ascontiguousarray(np.concatenate(blocks, axis=1))
    w2t = np.ascontiguousarray(np.asarray(w2, dtype=f32).T)
    nwb = np.ascontiguousarray(np.broadcast_to(
        np.asarray(norm_w, dtype=f32)[None, :], (128, FH)))
    m1q, m2q = _rope_factor_tables(_Q_PAIRS)
    m1k, m2k = _rope_factor_tables([(0, 1), (2, 3)])
    return dict(wq=wq, wqr=wqr, wkvc=wkvc, wkdec=wkdec, wvdec=wvdec, wkr=wkr,
                wo=wo, w13t=w13t_i, w2t=w2t, nwb=nwb,
                ident=np.eye(128, dtype=f32),
                identb=np.eye(128, dtype=ml_dtypes.bfloat16),
                dmask=_diag_masks(),
                ones=np.ones((128, 128), dtype=f32),
                m1q=m1q, m2q=m2q, m1k=m1k, m2k=m2k)


def kernel(L0_attn_hidden_states, L0_attn_Wq, L0_attn_Wkvc, L0_attn_Wkdec,
           L0_attn_Wvdec, L0_attn_Wqrope, L0_attn_Wkrope, L0_attn_Wo,
           L0_ffn_w13, L0_ffn_w2, L0_ffn_norm_w):
    x = np.ascontiguousarray(np.asarray(L0_attn_hidden_states, dtype=np.float32))
    B, S, _ = x.shape
    shared = _prep_shared(L0_attn_Wq, L0_attn_Wkvc, L0_attn_Wkdec, L0_attn_Wvdec,
                          L0_attn_Wqrope, L0_attn_Wkrope, L0_attn_Wo,
                          L0_ffn_w13, L0_ffn_w2, L0_ffn_norm_w, S)
    nc = build_core_program(S)
    in_maps = []
    for b in range(B):
        m = dict(shared)
        m["x"] = np.ascontiguousarray(x[b])
        in_maps.append(m)
    res = run_bass_kernel_spmd(nc, in_maps, core_ids=list(range(B)))
    return np.stack([np.asarray(res.results[b]["out"]) for b in range(B)], axis=0)
